# version 03
# baseline (speedup 1.0000x reference)
import contextlib
import numpy as np
import ml_dtypes
import concourse.bass as bass
import concourse.mybir as mybir
from concourse.bass_utils import run_bass_kernel_spmd

F32 = mybir.dt.float32
BF16 = mybir.dt.bfloat16
I32 = mybir.dt.int32
AF = mybir.ActivationFunctionType
ALU = mybir.AluOpType
BF = ml_dtypes.bfloat16

ENGINES = ("tensor", "vector", "scalar", "gpsimd", "sync")
D = 1024
TOK = 2048
TG = 512
NTG = 4
LOG1M = float(np.log1p(-1e-5))
TW = 6143
PW = 4223
PWS = 4224


class Prog:
    def __init__(self, nc, strict=False):
        self.nc = nc
        self.strict = strict
        self.ops = {e: [] for e in ENGINES}
        self.global_deps = []

    def add(self, eng, fn, deps=(), dma=False, tag=""):
        if self.strict:
            self.fence()
        deps = [d for d in deps if d is not None] + list(self.global_deps)
        i = len(self.ops[eng])
        self.ops[eng].append((fn, deps, dma, tag))
        return (eng, i)

    def fence(self):
        self.global_deps = [(e, len(self.ops[e]) - 1) for e in ENGINES if self.ops[e]]

    def emit(self, stack):
        nc = self.nc
        sems = {e: stack.enter_context(nc.semaphore(name=f"sem_{e}")) for e in ENGINES}
        awaited = set()
        for e in ENGINES:
            for i, (fn, deps, dma, tag) in enumerate(self.ops[e]):
                for (pe, pi) in deps:
                    if pe == e and pi >= i:
                        continue
                    if pe == "tensor" and e == "tensor":
                        continue
                    awaited.add((pe, pi))
        mile = {}
        counts = {e: 0 for e in ENGINES}
        for e in ENGINES:
            for i in range(len(self.ops[e])):
                if (e, i) in awaited or self.ops[e][i][2]:
                    counts[e] += 16 if self.ops[e][i][2] else 1
                    mile[(e, i)] = counts[e]

        def mk_stream(e):
            def stream(eng):
                waited = {}
                for i, (fn, deps, dma, tag) in enumerate(self.ops[e]):
                    for (pe, pi) in deps:
                        if pe == e and pi >= i:
                            continue
                        if pe == "tensor" and e == "tensor":
                            continue
                        v = mile.get((pe, pi))
                        if v is None:
                            raise RuntimeError(f"dep {(pe,pi)} not milestoned; consumer {(e,i)}:{tag}")
                        if waited.get(pe, 0) < v:
                            eng.wait_ge(sems[pe], v)
                            waited[pe] = v
                    ins = fn(eng)
                    if (e, i) in mile:
                        ins.then_inc(sems[e], 16 if dma else 1)
            return stream

        with nc.Block() as block:
            for e in ENGINES:
                if self.ops[e]:
                    getattr(block, e)(mk_stream(e))


def MM(out, lhsT, rhs, start, stop):
    return lambda e: e.matmul(out, lhsT, rhs, start=start, stop=stop)


def ACT(out, in_, func, bias=0.0, scale=1.0):
    return lambda e: e.activation(out, in_, func, bias=bias, scale=scale)


def TT(out, a, b, op):
    return lambda e: e.tensor_tensor(out, a, b, op)


def TS(out, in_, s1, s2, op0, op1):
    return lambda e: e.tensor_scalar(out, in_, s1, s2, op0, op1)


def TS1(out, in_, s, op):
    return lambda e: e.tensor_single_scalar(out, in_, s, op)


def STT(out, in0, s, in1, op0, op1):
    return lambda e: e.scalar_tensor_tensor(out, in0, s, in1, op0, op1)


def SCAN(out, d0, d1, init, op0, op1):
    return lambda e: e.tensor_tensor_scan(out, d0, d1, init, op0, op1)


def DMA(out, in_):
    return lambda e: e.dma_start(out=out, in_=in_)


def COPY(out, in_):
    return lambda e: e.tensor_copy(out, in_)


def SCOPY(out, in_):
    return lambda e: e.copy(out, in_)


def MEMSET(ap, v):
    return lambda e: e.memset(ap, v)


def emit_ln(p, bufs, src, srcdeps, dst, name, dst_bf16_tmp=None):
    sq, t1, t2, t3, tmpf, ones_bf, ones_f, ones1f, pbS1, pbS2, pbB1, pbB2 = bufs
    outd = {dc: [] for dc in range(8)}
    for tg in range(NTG):
        s = slice(tg * TG, (tg + 1) * TG)
        mm_s = mm_q = qprev = None
        for dc in range(8):
            q = p.add("scalar", ACT(sq[:, :], src[dc][:, s], AF.Square),
                      deps=srcdeps[dc] + [mm_q], tag=f"{name}q{tg}{dc}")
            mm_s = p.add("tensor", MM(pbS1[0:1, 0:TG], ones_f[:, 0:1], src[dc][:, s], dc == 0, dc == 7),
                         deps=srcdeps[dc], tag=f"{name}ms{tg}{dc}")
            mm_q = p.add("tensor", MM(pbS2[0:1, 0:TG], ones_f[:, 0:1], sq[:, :], dc == 0, dc == 7),
                         deps=[q], tag=f"{name}mq{tg}{dc}")
        a1 = p.add("vector", TS1(t1[:, :], pbS1[0:1, 0:TG], 1.0 / D, ALU.mult), deps=[mm_s], tag=f"{name}a1{tg}")
        a2 = p.add("vector", TS1(t2[:, :], pbS2[0:1, 0:TG], 1.0 / D, ALU.mult), deps=[mm_q], tag=f"{name}a2{tg}")
        a3 = p.add("vector", TT(t3[:, :], t1[:, :], t1[:, :], ALU.mult), deps=[a1], tag=f"{name}a3{tg}")
        a4 = p.add("vector", TT(t2[:, :], t2[:, :], t3[:, :], ALU.subtract), deps=[a2, a3], tag=f"{name}a4{tg}")
        r1 = p.add("scalar", ACT(t2[:, :], t2[:, :], AF.Ln, bias=1e-5), deps=[a4], tag=f"{name}r1{tg}")
        r2 = p.add("scalar", ACT(t2[:, :], t2[:, :], AF.Exp, scale=-0.5), deps=[r1], tag=f"{name}r2{tg}")
        nm = p.add("vector", STT(t3[:, :], t1[:, :], -1.0, t2[:, :], ALU.mult, ALU.mult),
                   deps=[a3, r2], tag=f"{name}nm{tg}")
        b1 = p.add("tensor", MM(pbB1[:, 0:TG], ones1f[0:1, :], t2[0:1, :], True, True), deps=[r2], tag=f"{name}b1{tg}")
        b2 = p.add("tensor", MM(pbB2[:, 0:TG], ones1f[0:1, :], t3[0:1, :], True, True), deps=[nm], tag=f"{name}b2{tg}")
        prev = None
        for dc in range(8):
            w1 = p.add("vector", TT(tmpf[:, :], src[dc][:, s], pbB1[:, 0:TG], ALU.mult),
                       deps=srcdeps[dc] + [b1, prev], tag=f"{name}w1{tg}{dc}")
            w2 = p.add("vector", TT(dst[dc][:, s], tmpf[:, :], pbB2[:, 0:TG], ALU.add),
                       deps=[w1, b2], tag=f"{name}w2{tg}{dc}")
            outd[dc].append(w2)
            prev = w2
        p.fence()
    return outd


def emit_ln_stream(p, bufs, xg, x_dram, dst, name):
    sq, t1, t2, t3, tmpf, ones_bf, ones_f, ones1f, pbS1, pbS2, pbB1, pbB2 = bufs
    outd = {dc: [] for dc in range(8)}
    for tg in range(NTG):
        s = slice(tg * TG, (tg + 1) * TG)
        ldd = {dc: [p.add("sync", DMA(xg[dc][:, :], x_dram[dc * 128:(dc + 1) * 128, s]),
                          dma=True, tag=f"{name}ld{tg}{dc}")] for dc in range(8)}
        mm_s = mm_q = None
        for dc in range(8):
            q = p.add("scalar", ACT(sq[:, :], xg[dc][:, :], AF.Square),
                      deps=ldd[dc] + [mm_q], tag=f"{name}q{tg}{dc}")
            mm_s = p.add("tensor", MM(pbS1[0:1, 0:TG], ones_f[:, 0:1], xg[dc][:, :], dc == 0, dc == 7),
                         deps=ldd[dc], tag=f"{name}ms{tg}{dc}")
            mm_q = p.add("tensor", MM(pbS2[0:1, 0:TG], ones_f[:, 0:1], sq[:, :], dc == 0, dc == 7),
                         deps=[q], tag=f"{name}mq{tg}{dc}")
        a1 = p.add("vector", TS1(t1[:, :], pbS1[0:1, 0:TG], 1.0 / D, ALU.mult), deps=[mm_s], tag=f"{name}a1{tg}")
        a2 = p.add("vector", TS1(t2[:, :], pbS2[0:1, 0:TG], 1.0 / D, ALU.mult), deps=[mm_q], tag=f"{name}a2{tg}")
        a3 = p.add("vector", TT(t3[:, :], t1[:, :], t1[:, :], ALU.mult), deps=[a1], tag=f"{name}a3{tg}")
        a4 = p.add("vector", TT(t2[:, :], t2[:, :], t3[:, :], ALU.subtract), deps=[a2, a3], tag=f"{name}a4{tg}")
        r1 = p.add("scalar", ACT(t2[:, :], t2[:, :], AF.Ln, bias=1e-5), deps=[a4], tag=f"{name}r1{tg}")
        r2 = p.add("scalar", ACT(t2[:, :], t2[:, :], AF.Exp, scale=-0.5), deps=[r1], tag=f"{name}r2{tg}")
        nm = p.add("vector", STT(t3[:, :], t1[:, :], -1.0, t2[:, :], ALU.mult, ALU.mult),
                   deps=[a3, r2], tag=f"{name}nm{tg}")
        b1 = p.add("tensor", MM(pbB1[:, 0:TG], ones1f[0:1, :], t2[0:1, :], True, True), deps=[r2], tag=f"{name}b1{tg}")
        b2 = p.add("tensor", MM(pbB2[:, 0:TG], ones1f[0:1, :], t3[0:1, :], True, True), deps=[nm], tag=f"{name}b2{tg}")
        prev = None
        for dc in range(8):
            w1 = p.add("vector", TT(tmpf[:, :], xg[dc][:, :], pbB1[:, 0:TG], ALU.mult),
                       deps=ldd[dc] + [b1, prev], tag=f"{name}w1{tg}{dc}")
            w2 = p.add("vector", TT(dst[dc][:, s], tmpf[:, :], pbB2[:, 0:TG], ALU.add),
                       deps=[w1, b2], tag=f"{name}w2{tg}{dc}")
            outd[dc].append(w2)
            prev = w2
        p.fence()
    return outd


def mk_common(nc, stack, p, uid=""):
    def sb(name, shape, dt):
        return stack.enter_context(nc.sbuf_tensor(uid + name, shape, dt)).ap()

    def ps(n):
        return stack.enter_context(nc.psum_tensor(uid + n, [128, 512], F32)).ap()

    ones_bf = sb("c_ones_bf", [128, 1], BF16)
    ones_f = sb("c_ones_f", [128, 1], F32)
    ones1f = sb("c_ones1f", [1, 128], F32)
    sq = sb("c_sq", [128, TG], F32)
    t1 = sb("c_t1", [1, TG], F32)
    t2 = sb("c_t2", [1, TG], F32)
    t3 = sb("c_t3", [1, TG], F32)
    tmpf = sb("c_tmpf", [128, TG], F32)
    p.add("vector", MEMSET(ones_bf[:, :], 1.0), tag="c_ones")
    p.add("vector", MEMSET(ones_f[:, :], 1.0), tag="c_onesf")
    p.add("vector", MEMSET(ones1f[:, :], 1.0), tag="c_ones1")
    cst = sb("c_cst", [128, 2], F32)
    p.add("vector", MEMSET(cst[:, 0:1], 1e-5), tag="c_c0")
    p.add("vector", MEMSET(cst[:, 1:2], 1.0), tag="c_c1")
    if not getattr(nc, "_cst_done", False):
        nc.const_aps.register(F32, 1e-5, cst[:, 0:1])
        nc.const_aps.register(F32, 1.0, cst[:, 1:2])
        nc._cst_done = True
    p.fence()
    return sb, ps, ones_bf, ones1f, ones_f, (sq, t1, t2, t3, tmpf)


class Ctx:
    def __init__(self, nc, p, dram_cache):
        self.nc, self.p, self.dcache = nc, p, dram_cache

    def dram(self, name, shape, dt, kind):
        if name not in self.dcache:
            self.dcache[name] = self.nc.dram_tensor(name, shape, dt, kind=kind).ap()
        return self.dcache[name]


def build_ffn0(ctx=None, uid="", xin="x_in", xout="x_out", dq=True):
    own = ctx is None
    if own:
        nc = bass.Bass("TRN2", target_bir_lowering=False)
        p = Prog(nc, strict=True)
        ctx = Ctx(nc, p, {})
    nc, p = ctx.nc, ctx.p
    x_in = ctx.dram(xin, [D, TOK], F32, "ExternalInput" if xin == "x_in" else "Internal")
    qT = ctx.dram("qT", [6, D, D], I32, "ExternalInput")
    srT = ctx.dram("srT", [6, D, D], F32, "ExternalInput")
    laT = ctx.dram("laT", [6, D, 32], F32, "ExternalInput")
    lbT = ctx.dram("lbT", [6, 32, D], F32, "ExternalInput")
    pvec = ctx.dram("pvec", [128, 96], F32, "ExternalInput")
    x_out = ctx.dram(xout, [D, TOK], F32, "ExternalOutput" if xout == "x_out" else "Internal")
    wdram = ctx.dram("p1_wdram", [6, D, D], BF16, "Internal")

    stack = contextlib.ExitStack()
    sb, ps, ones_bf, ones1f, ones_f, (sq, t1, t2, t3, tmpf) = mk_common(nc, stack, p, uid + "f0")

    h = [sb(f"h{i}", [128, TOK], F32) for i in range(8)]
    hb = [sb(f"hb{i}", [128, TG], BF16) for i in range(8)]
    gb = [sb(f"gb{i}", [128, TG], BF16) for i in range(8)]
    y1b = [sb(f"y1b{i}", [128, TG], F32) for i in range(8)]
    wcur = [sb(f"wc{i}", [128, 2 * D], BF16) for i in range(8)]
    la_sb = sb("la_sb", [128, 6 * 8 * 32], BF16)
    lb_sb = sb("lb_sb", [32, 6 * D], BF16)
    pv = sb("pv", [128, 96], F32)
    xtile = sb("xtile", [128, TG], F32)
    qtile = [sb(f"qt{i}", [128, D], I32) for i in range(2)]
    srtile = [sb(f"st{i}", [128, D], F32) for i in range(2)]
    wtmp = sb("wtmp", [128, D], F32)
    wq_bf = [sb(f"wqb{i}", [128, D], BF16) for i in range(2)]
    rb = sb("rb", [32, TG], BF16)
    ftmp = sb("ftmp", [128, TG], F32)
    ftmp2 = sb("ftmp2", [128, TG], F32)
    ls32 = sb("ls32", [128, 32], F32)
    lbtmp = sb("lbtmp", [32, D], F32)

    pbY = [ps("pbY0"), ps("pbY1")]
    pbR = ps("pbR")
    pbS1, pbS2, pbB1, pbB2 = ps("pbS1"), ps("pbS2"), ps("pbB1"), ps("pbB2")
    lnbufs = (sq, t1, t2, t3, tmpf, ones_bf, ones_f, ones1f, pbS1, pbS2, pbB1, pbB2)

    p.add("sync", DMA(pv[:, :], pvec[:, :]), dma=True, tag="ldpv")
    for m in range(6):
        for dc in range(8):
            l1 = p.add("sync", DMA(ls32[:, :], laT[m, dc * 128:(dc + 1) * 128, :]), dma=True, tag=f"lla{m}{dc}")
            p.add("vector", COPY(la_sb[:, (m * 8 + dc) * 32:(m * 8 + dc + 1) * 32], ls32[:, :]),
                  deps=[l1], tag=f"cla{m}{dc}")
            p.fence()
        l2 = p.add("sync", DMA(lbtmp[:, :], lbT[m, :, :]), dma=True, tag=f"llb{m}")
        p.add("vector", COPY(lb_sb[:, m * D:(m + 1) * D], lbtmp[:, :]), deps=[l2], tag=f"clb{m}")
        p.fence()

    for m in range(6 if dq else 0):
        for dc in range(8):
            bi = dc % 2
            lq = p.add("sync", DMA(qtile[bi][:, :], qT[m, dc * 128:(dc + 1) * 128, :]), dma=True, tag=f"lq{m}{dc}")
            lsr = p.add("sync", DMA(srtile[bi][:, :], srT[m, dc * 128:(dc + 1) * 128, :]), dma=True, tag=f"lsr{m}{dc}")
            cc = p.add("vector", COPY(wtmp[:, :], qtile[bi][:, :]), deps=[lq], tag=f"dc{m}{dc}")
            ca = p.add("vector", TS(wtmp[:, :], wtmp[:, :], 1.0 / 7.5, -1.0, ALU.mult, ALU.add),
                       deps=[cc], tag=f"da{m}{dc}")
            cm = p.add("vector", TT(wq_bf[bi][:, :], wtmp[:, :], srtile[bi][:, :], ALU.mult),
                       deps=[ca, lsr], tag=f"dm{m}{dc}")
            p.add("sync", DMA(wdram[m, dc * 128:(dc + 1) * 128, :], wq_bf[bi][:, :]),
                  deps=[cm], dma=True, tag=f"dw{m}{dc}")
            if bi == 1:
                p.fence()
    p.fence()

    xdeps = {dc: [p.add("sync", DMA(h[dc][:, :], x_in[dc * 128:(dc + 1) * 128, :]), dma=True, tag=f"lx{dc}")]
             for dc in range(8)}
    p.fence()
    emit_ln(p, lnbufs, h, xdeps, h, "ln0")
    p.fence()

    h_w = {dc: [] for dc in range(8)}
    for blk in range(3):
        m1, m2 = 2 * blk, 2 * blk + 1
        wl = {}
        for dc in range(8):
            wl[dc] = [
                p.add("sync", DMA(wcur[dc][:, 0:D], wdram[m1, dc * 128:(dc + 1) * 128, :]), dma=True, tag=f"wA{blk}{dc}"),
                p.add("sync", DMA(wcur[dc][:, D:2 * D], wdram[m2, dc * 128:(dc + 1) * 128, :]), dma=True, tag=f"wB{blk}{dc}"),
            ]
        p.fence()
        for tg in range(NTG):
            s = slice(tg * TG, (tg + 1) * TG)
            cb = {dc: p.add("scalar", ACT(hb[dc][:, :], h[dc][:, s], AF.Copy), deps=h_w[dc], tag=f"cb{blk}{tg}{dc}")
                  for dc in range(8)}
            mm = None
            for dc in range(8):
                mm = p.add("tensor", MM(pbR[0:32, 0:TG], la_sb[:, (m1 * 8 + dc) * 32:(m1 * 8 + dc + 1) * 32],
                                        hb[dc][:, :], dc == 0, dc == 7), deps=[cb[dc]], tag=f"r1{blk}{tg}{dc}")
            er = p.add("scalar", ACT(rb[:, :], pbR[0:32, 0:TG], AF.Copy), deps=[mm], tag=f"er1{blk}{tg}")
            ev_prev = {0: None, 1: None}
            for oc in range(8):
                y = pbY[oc % 2]
                for dc in range(8):
                    p.add("tensor", MM(y[:, 0:TG], wcur[dc][:, oc * 128:(oc + 1) * 128],
                                       hb[dc][:, :], dc == 0, False),
                          deps=[cb[dc], wl[dc][0], ev_prev[oc % 2]], tag=f"y1{blk}{tg}{oc}{dc}")
                mmy = p.add("tensor", MM(y[:, 0:TG], lb_sb[:, m1 * D + oc * 128:m1 * D + (oc + 1) * 128],
                                         rb[:, :], False, True), deps=[er], tag=f"y1l{blk}{tg}{oc}")
                ev = p.add("scalar", ACT(y1b[oc][:, :], y[:, 0:TG], AF.Identity,
                                         bias=pv[:, m1 * 8 + oc:m1 * 8 + oc + 1]),
                           deps=[mmy], tag=f"ey1{blk}{tg}{oc}")
                ev_prev[oc % 2] = ev
            evs = [ev_prev[0], ev_prev[1]]
            mm_s = mm_q = None
            for dc in range(8):
                q = p.add("scalar", ACT(sq[:, :], y1b[dc][:, :], AF.Square),
                          deps=evs + [mm_q], tag=f"isq{blk}{tg}{dc}")
                mm_s = p.add("tensor", MM(pbS1[0:1, 0:TG], ones_f[:, 0:1], y1b[dc][:, :], dc == 0, dc == 7),
                             deps=evs, tag=f"ims{blk}{tg}{dc}")
                mm_q = p.add("tensor", MM(pbS2[0:1, 0:TG], ones_f[:, 0:1], sq[:, :], dc == 0, dc == 7),
                             deps=[q], tag=f"imq{blk}{tg}{dc}")
            a1 = p.add("vector", TS1(t1[:, :], pbS1[0:1, 0:TG], 1.0 / D, ALU.mult), deps=[mm_s], tag=f"ia1{blk}{tg}")
            a2 = p.add("vector", TS1(t2[:, :], pbS2[0:1, 0:TG], 1.0 / D, ALU.mult), deps=[mm_q], tag=f"ia2{blk}{tg}")
            a3 = p.add("vector", TT(t3[:, :], t1[:, :], t1[:, :], ALU.mult), deps=[a1], tag=f"ia3{blk}{tg}")
            a4 = p.add("vector", TT(t2[:, :], t2[:, :], t3[:, :], ALU.subtract), deps=[a2, a3], tag=f"ia4{blk}{tg}")
            r1 = p.add("scalar", ACT(t2[:, :], t2[:, :], AF.Ln, bias=1e-5), deps=[a4], tag=f"ir1{blk}{tg}")
            r2 = p.add("scalar", ACT(t2[:, :], t2[:, :], AF.Exp, scale=-0.5), deps=[r1], tag=f"ir2{blk}{tg}")
            nm = p.add("vector", STT(t3[:, :], t1[:, :], -1.0, t2[:, :], ALU.mult, ALU.mult),
                       deps=[a3, r2], tag=f"inm{blk}{tg}")
            b1 = p.add("tensor", MM(pbB1[:, 0:TG], ones1f[0:1, :], t2[0:1, :], True, True), deps=[r2], tag=f"ib1{blk}{tg}")
            b2 = p.add("tensor", MM(pbB2[:, 0:TG], ones1f[0:1, :], t3[0:1, :], True, True), deps=[nm], tag=f"ib2{blk}{tg}")
            gd = {}
            prev = None
            for dc in range(8):
                w1 = p.add("vector", TT(ftmp[:, :], y1b[dc][:, :], pbB1[:, 0:TG], ALU.mult),
                           deps=evs + [b1, prev], tag=f"iw1{blk}{tg}{dc}")
                w2 = p.add("vector", TT(ftmp[:, :], ftmp[:, :], pbB2[:, 0:TG], ALU.add), deps=[w1, b2], tag=f"iw2{blk}{tg}{dc}")
                w3 = p.add("vector", TS(ftmp[:, :], ftmp[:, :], pv[:, 48 + blk * 8 + dc:48 + blk * 8 + dc + 1],
                                        pv[:, 72 + blk * 8 + dc:72 + blk * 8 + dc + 1], ALU.mult, ALU.add),
                           deps=[w2], tag=f"iw3{blk}{tg}{dc}")
                g = p.add("scalar", ACT(gb[dc][:, :], ftmp[:, :], AF.Gelu), deps=[w3], tag=f"ig{blk}{tg}{dc}")
                gd[dc] = g
                prev = g
            mm = None
            for dc in range(8):
                mm = p.add("tensor", MM(pbR[0:32, 0:TG], la_sb[:, (m2 * 8 + dc) * 32:(m2 * 8 + dc + 1) * 32],
                                        gb[dc][:, :], dc == 0, dc == 7), deps=[gd[dc], er], tag=f"r2{blk}{tg}{dc}")
            er2 = p.add("scalar", ACT(rb[:, :], pbR[0:32, 0:TG], AF.Copy), deps=[mm], tag=f"er2{blk}{tg}")
            evp2 = {0: evs[0], 1: evs[1]}
            for oc in range(8):
                y = pbY[oc % 2]
                for dc in range(8):
                    p.add("tensor", MM(y[:, 0:TG], wcur[dc][:, D + oc * 128:D + (oc + 1) * 128],
                                       gb[dc][:, :], dc == 0, False),
                          deps=[gd[dc], wl[dc][1], evp2[oc % 2]], tag=f"y2{blk}{tg}{oc}{dc}")
                mmy = p.add("tensor", MM(y[:, 0:TG], lb_sb[:, m2 * D + oc * 128:m2 * D + (oc + 1) * 128],
                                         rb[:, :], False, True), deps=[er2], tag=f"y2l{blk}{tg}{oc}")
                u1 = p.add("vector", TS1(ftmp2[:, :], y[:, 0:TG], pv[:, m2 * 8 + oc:m2 * 8 + oc + 1], ALU.add),
                           deps=[mmy], tag=f"u1{blk}{tg}{oc}")
                u2 = p.add("vector", TT(h[oc][:, s], ftmp2[:, :], h[oc][:, s], ALU.add),
                           deps=[u1] + h_w[oc], tag=f"u2{blk}{tg}{oc}")
                h_w[oc] = [u2]
                evp2[oc % 2] = u1
            p.fence()

    for tg in range(NTG):
        s = slice(tg * TG, (tg + 1) * TG)
        for dc in range(8):
            lx = p.add("sync", DMA(xtile[:, :], x_in[dc * 128:(dc + 1) * 128, s]), dma=True, tag=f"fx{tg}{dc}")
            fo = p.add("vector", STT(xtile[:, :], h[dc][:, s], 0.5, xtile[:, :], ALU.mult, ALU.add),
                       deps=[lx] + h_w[dc], tag=f"ff{tg}{dc}")
            st = p.add("sync", DMA(x_out[dc * 128:(dc + 1) * 128, s], xtile[:, :]), deps=[fo], dma=True, tag=f"fs{tg}{dc}")
            p.fence()
    if own:
        p.add("sync", lambda e: e.nop(), tag="final")
        p.emit(stack)
        stack.close()
        return nc
    p.fence()
    stack.close()
    return None


def build_scan(ctx=None, uid="", xin="x_in", xout="x_out"):
    import os
    CUT = 5
    own = ctx is None
    if own:
        nc = bass.Bass("TRN2", target_bir_lowering=False)
        p = Prog(nc, strict=True)
        ctx = Ctx(nc, p, {})
    nc, p = ctx.nc, ctx.p
    x_in = ctx.dram(xin, [D, TOK], F32, "ExternalInput" if xin == "x_in" else "Internal")
    w_in = ctx.dram("w_in", [8, 128, 128], F32, "ExternalInput")
    w_a = ctx.dram("w_a", [8, 128, 128], F32, "ExternalInput")
    pvec = ctx.dram("pvec2", [128, 24], F32, "ExternalInput")
    selm = ctx.dram("selm", [1, 1], F32, "ExternalInput")
    x_out = ctx.dram(xout, [D, TOK], F32, "ExternalOutput" if xout == "x_out" else "Internal")
    dbg = ctx.dram("dbg", [128, 6 * TOK], F32, "Internal")
    ccLi = ctx.dram("ccLi", [1, D], F32, "Internal")
    ccLo = ctx.dram("ccLo", [2, D], F32, "Internal")
    ccSi = ctx.dram("ccSi", [1, D], F32, "Internal")
    ccSo = ctx.dram("ccSo", [2, D], F32, "Internal")

    stack = contextlib.ExitStack()
    sb, ps, ones_bf, ones1f, ones_f, (sq, t1, t2, t3, tmpf) = mk_common(nc, stack, p, uid + "sc")

    xn = [sb(f"xn{i}", [128, TOK], F32) for i in range(8)]
    LX = [sb(f"LX{i}", [128, TOK], F32) for i in range(8)]
    wsb = sb("wsb", [128, 2 * 128], F32)
    pv = sb("pv2", [128, 24], F32)
    gx = sb("gx", [128, TOK], F32)
    ga = sb("ga", [128, TOK], F32)
    la = sb("la", [128, TOK], F32)
    at = sb("at", [128, TOK], F32)
    wt = sb("wt", [128, TOK], F32)
    ones_t = sb("ones_t", [128, TOK], F32)
    negkd = sb("negkd", [128, 8], F32)
    kda = sb("kda", [128, 8], F32)
    kdb = sb("kdb", [128, 8], F32)
    sel_sb = sb("sel_sb", [1, 1], F32)
    gl = sb("gl", [2, D], F32)
    cstrip = sb("cstrip", [1, D], F32)
    ccol = sb("ccol", [128, 8 * 2], F32)
    scol = sb("scol", [128, 8], F32)
    onef1 = sb("onef1", [1, 1], F32)
    xt2 = sb("xt2", [128, TG], F32)

    pbG, pbA = ps("pbG"), ps("pbA")
    pbS1, pbS2, pbB1, pbB2 = ps("pbS1b"), ps("pbS2b"), ps("pbB1b"), ps("pbB2b")
    lnbufs = (sq, t1, t2, t3, tmpf, ones_bf, ones_f, ones1f, pbS1, pbS2, pbB1, pbB2)

    p.add("vector", MEMSET(ones_t[:, :], 1.0), tag="onest")
    p.add("vector", MEMSET(onef1[:, :], 1.0), tag="onef1")
    lpv = p.add("sync", DMA(pv[:, :], pvec[:, :]), dma=True, tag="lpv")
    p.add("sync", DMA(sel_sb[:, :], selm[:, :]), dma=True, tag="lsel")
    s1 = p.add("scalar", ACT(kda[:, :], pv[:, 16:24], AF.Abs), deps=[lpv], tag="sp1")
    s2 = p.add("scalar", ACT(kda[:, :], kda[:, :], AF.Exp, scale=-1.0), deps=[s1], tag="sp2")
    s3 = p.add("scalar", ACT(kda[:, :], kda[:, :], AF.Ln, bias=1.0), deps=[s2], tag="sp3")
    s4 = p.add("vector", TS1(kdb[:, :], pv[:, 16:24], 0.0, ALU.max), deps=[lpv], tag="sp4")
    s5 = p.add("vector", TT(kda[:, :], kda[:, :], kdb[:, :], ALU.add), deps=[s3, s4], tag="sp5")
    s6 = p.add("vector", TS(negkd[:, :], kda[:, :], -1.0, -0.001, ALU.mult, ALU.add), deps=[s5], tag="sp6")
    p.fence()

    xdeps = {dc: [p.add("sync", DMA(xn[dc][:, :], x_in[dc * 128:(dc + 1) * 128, :]), dma=True, tag=f"lx{dc}")]
             for dc in range(8)}
    p.fence()
    emit_ln(p, lnbufs, xn, xdeps, xn, "ln2")
    p.fence()

    for hc in range(8 if CUT >= 2 else 0):
        cw1 = p.add("sync", DMA(wsb[:, 0:128], w_in[hc, :, :]), dma=True, tag=f"lw1{hc}")
        cw2 = p.add("sync", DMA(wsb[:, 128:256], w_a[hc, :, :]), deps=[cw1], dma=True, tag=f"lw2{hc}")
        for tg in range(NTG):
            s = slice(tg * TG, (tg + 1) * TG)
            m1 = p.add("tensor", MM(pbG[:, 0:TG], wsb[:, 0:128], xn[hc][:, s], True, True),
                       deps=[cw1], tag=f"mg{hc}{tg}")
            e1 = p.add("scalar", ACT(gx[:, s], pbG[:, 0:TG], AF.Sigmoid, bias=pv[:, hc:hc + 1]),
                       deps=[m1], tag=f"eg{hc}{tg}")
            m2 = p.add("tensor", MM(pbA[:, 0:TG], wsb[:, 128:256], xn[hc][:, s], True, True),
                       deps=[cw2], tag=f"ma{hc}{tg}")
            e2 = p.add("scalar", ACT(ga[:, s], pbA[:, 0:TG], AF.Sigmoid, bias=pv[:, 8 + hc:8 + hc + 1]),
                       deps=[m2], tag=f"ea{hc}{tg}")
            p.fence()
        v1a = p.add("vector", TS1(la[:, :], ga[:, :], negkd[:, hc:hc + 1], ALU.mult),
                    deps=[s6], tag=f"vlaa{hc}")
        v1 = p.add("vector", TS1(la[:, :], la[:, :], LOG1M, ALU.min),
                   deps=[v1a], tag=f"vla{hc}")
        v2 = p.add("vector", SCAN(LX[hc][:, :], ones_t[:, :], la[:, :], 0.0, ALU.mult, ALU.add),
                   deps=[v1], tag=f"scL{hc}")
        ctb = p.add("sync", DMA(ccLi[0:1, hc * 128:(hc + 1) * 128], LX[hc][:, TOK - 1:TOK]),
                    deps=[v2], dma=True, tag=f"ctL{hc}")
        v3 = p.add("vector", TT(LX[hc][:, :], LX[hc][:, :], la[:, :], ALU.subtract), deps=[v2, ctb], tag=f"exL{hc}")
        a1 = p.add("scalar", ACT(at[:, :], la[:, :], AF.Exp), deps=[v1, v3], tag=f"ea2{hc}")
        a2 = p.add("scalar", ACT(at[:, :], at[:, :], AF.Square), deps=[a1], tag=f"ea3{hc}")
        a3 = p.add("vector", TS(at[:, :], at[:, :], -1.0, 1.0, ALU.mult, ALU.add), deps=[a2], tag=f"ea4{hc}")
        a4 = p.add("vector", TS1(at[:, :], at[:, :], 1e-5, ALU.max), deps=[a3], tag=f"ea5{hc}")
        a5 = p.add("scalar", ACT(at[:, :], at[:, :], AF.Sqrt), deps=[a4], tag=f"ea6{hc}")
        if hc == 0:
            p.add("sync", DMA(dbg[:, 0:TOK], gx[:, :]), deps=[e1], dma=True, tag="dbg0")
            p.add("sync", DMA(dbg[:, TOK:2 * TOK], ga[:, :]), deps=[e2], dma=True, tag="dbg1")
            p.add("sync", DMA(dbg[:, 2 * TOK:3 * TOK], la[:, :]), deps=[v1], dma=True, tag="dbg2")
            p.add("sync", DMA(dbg[:, 3 * TOK:4 * TOK], LX[0][:, :]), deps=[v3], dma=True, tag="dbg3")
        u1 = p.add("vector", TT(xn[hc][:, :], xn[hc][:, :], gx[:, :], ALU.mult), deps=[a5, e1], tag=f"u1{hc}")
        u2 = p.add("vector", TT(xn[hc][:, :], xn[hc][:, :], at[:, :], ALU.mult), deps=[u1], tag=f"u2{hc}")
        if hc == 0:
            p.add("sync", DMA(dbg[:, 4 * TOK:5 * TOK], xn[0][:, :]), deps=[u2], dma=True, tag="dbg4")
            p.add("sync", DMA(dbg[:, 5 * TOK:6 * TOK], at[:, :]), deps=[a5, u2], dma=True, tag="dbg5")
        p.fence()

    if CUT >= 3:
        if os.environ.get("SCAN_NOCC", "0") == "1":
            ccL = p.add("gpsimd", lambda e: e.nop(), tag="ccLnop")
        else:
            ccL = p.add("gpsimd", lambda e: e.collective_compute(
                "AllGather", ALU.bypass, replica_groups=[[0, 1], [2, 3], [4, 5], [6, 7]],
                ins=[ccLi[:, :]], outs=[ccLo[:, :]]), tag="ccL")
        lgl = p.add("sync", DMA(gl[:, :], ccLo[:, :]), deps=[ccL], dma=True, tag="lgl")
        sc0 = p.add("vector", TS1(cstrip[0:1, :], gl[0:1, :], sel_sb[0:1, 0:1], ALU.mult),
                    deps=[lgl], tag="selL")
        p.fence()
        cold = []
        for hc in range(8):
            cd = p.add("sync", DMA(ccol[:, 2 * hc:2 * hc + 1], cstrip[0:1, hc * 128:(hc + 1) * 128]),
                       deps=[sc0], dma=True, tag=f"cL{hc}")
            cold.append(cd)
        p.fence()
        for hc in range(8):
            p.add("vector", TS1(ccol[:, 2 * hc + 1:2 * hc + 2], ccol[:, 2 * hc:2 * hc + 1], -1.0, ALU.mult),
                  tag=f"cLn{hc}")
        p.fence()
    else:
        p.add("vector", MEMSET(ccol[:, :], 0.0), tag="zc")
        p.fence()

    ECL = 16.118096
    for hc in range(8 if CUT >= 4 else 0):
        lt0 = p.add("vector", TS1(la[:, :], LX[hc][:, :], ccol[:, 2 * hc:2 * hc + 1], ALU.add), tag=f"lt{hc}")
        lc1 = p.add("vector", TS1(at[:, :], la[:, :], -ECL, ALU.max), deps=[lt0], tag=f"lc1{hc}")
        pe = p.add("scalar", ACT(gx[:, :], at[:, :], AF.Exp), deps=[lc1], tag=f"pe{hc}")
        pc = p.add("vector", TS1(gx[:, :], gx[:, :], 1e-7, ALU.max), deps=[pe], tag=f"pc{hc}")
        lc2 = p.add("vector", TS(at[:, :], la[:, :], -1.0, ECL, ALU.mult, ALU.min), deps=[lt0, pe], tag=f"lc2{hc}")
        pi = p.add("scalar", ACT(ga[:, :], at[:, :], AF.Exp), deps=[lc2], tag=f"pi{hc}")
        pic = p.add("vector", TS1(ga[:, :], ga[:, :], 1e7, ALU.min), deps=[pi], tag=f"pic{hc}")
        w1 = p.add("vector", STT(wt[:, :], xn[hc][:, :], 1e-7, ga[:, :], ALU.add, ALU.mult),
                   deps=[pic], tag=f"w{hc}")
        sc = p.add("vector", SCAN(xn[hc][:, :], ones_t[:, :], wt[:, :], 0.0, ALU.mult, ALU.add),
                   deps=[w1], tag=f"scS{hc}")
        p.add("sync", DMA(ccSi[0:1, hc * 128:(hc + 1) * 128], xn[hc][:, TOK - 1:TOK]), deps=[sc],
              dma=True, tag=f"ctS{hc}")
        p.add("vector", COPY(LX[hc][:, :], gx[:, :]), deps=[pc, sc], tag=f"keepp{hc}")
        p.fence()

    if CUT >= 5:
        ccS = p.add("gpsimd", lambda e: e.collective_compute(
            "AllGather", ALU.bypass, replica_groups=[[0, 1], [2, 3], [4, 5], [6, 7]],
            ins=[ccSi[:, :]], outs=[ccSo[:, :]]), tag="ccS")
        lgs = p.add("sync", DMA(gl[:, :], ccSo[:, :]), deps=[ccS], dma=True, tag="lgs")
        sc1 = p.add("vector", TS1(cstrip[0:1, :], gl[0:1, :], sel_sb[0:1, 0:1], ALU.mult),
                    deps=[lgs], tag="selS")
        p.fence()
        for hc in range(8):
            p.add("sync", DMA(scol[:, hc:hc + 1], cstrip[0:1, hc * 128:(hc + 1) * 128]),
                  deps=[sc1], dma=True, tag=f"cS{hc}")
        p.fence()
    else:
        p.add("vector", MEMSET(scol[:, :], 0.0), tag="zs")
        p.fence()

    for hc in range(8 if CUT >= 5 else 0):
        hf = p.add("vector", STT(xn[hc][:, :], xn[hc][:, :], scol[:, hc:hc + 1], LX[hc][:, :],
                                 ALU.add, ALU.mult), tag=f"hf{hc}")
        for tg in range(NTG):
            s = slice(tg * TG, (tg + 1) * TG)
            lx = p.add("sync", DMA(xt2[:, :], x_in[hc * 128:(hc + 1) * 128, s]), dma=True, tag=f"fx{hc}{tg}")
            fo = p.add("vector", TT(xt2[:, :], xt2[:, :], xn[hc][:, s], ALU.add), deps=[lx, hf], tag=f"fa{hc}{tg}")
            p.add("sync", DMA(x_out[hc * 128:(hc + 1) * 128, s], xt2[:, :]), deps=[fo], dma=True, tag=f"fs{hc}{tg}")
            p.fence()
    if own:
        p.add("sync", lambda e: e.nop(), tag="final")
        p.emit(stack)
        stack.close()
        return nc
    p.fence()
    stack.close()
    return None


def build_ffn1(ctx=None, uid="", xin="x_in", xout="x_out", dq=True):
    HID = 3072
    own = ctx is None
    if own:
        nc = bass.Bass("TRN2", target_bir_lowering=False)
        p = Prog(nc, strict=True)
        ctx = Ctx(nc, p, {})
    nc, p = ctx.nc, ctx.p
    x_in = ctx.dram(xin, [D, TOK], F32, "ExternalInput" if xin == "x_in" else "Internal")
    wgT = ctx.dram("wgT", [D, HID], I32, "ExternalInput")
    wuT = ctx.dram("wuT", [D, HID], I32, "ExternalInput")
    wdT = ctx.dram("wdT", [HID, D], I32, "ExternalInput")
    x_out = ctx.dram(xout, [D, TOK], F32, "ExternalOutput" if xout == "x_out" else "Internal")
    wd_dram = ctx.dram("p3_wd", [HID, D], BF16, "Internal")
    wg_dram = ctx.dram("p3_wg", [D, HID], BF16, "Internal")
    wu_dram = ctx.dram("p3_wu", [D, HID], BF16, "Internal")

    stack = contextlib.ExitStack()
    sb, ps, ones_bf, ones1f, ones_f, (sq, t1, t2, t3, tmpf) = mk_common(nc, stack, p, uid + "f1")

    xg = [sb(f"xg{i}", [128, TG], F32) for i in range(8)]
    xnb = [sb(f"xb{i}", [128, TOK], BF16) for i in range(8)]
    wgb = [sb(f"wg{i}", [128, HID], BF16) for i in range(8)]
    wub = [sb(f"wu{i}", [128, HID], BF16) for i in range(8)]
    hid = [sb(f"hd{i}", [128, TG], BF16) for i in range(24)]
    itile = sb("it", [128, HID], I32)
    ftile = sb("ft", [128, D], F32)
    wdt = [sb(f"wdt{i}", [128, 128], BF16) for i in range(2)]
    sg = sb("sg", [128, TG], F32)
    xt = sb("xt", [128, TG], F32)
    btmp = sb("btmp", [128, D], BF16)

    pbG, pbU = ps("pbG"), ps("pbU")
    pbD = [ps("pbD0"), ps("pbD1")]
    pbS1, pbS2, pbB1, pbB2 = ps("pbS1c"), ps("pbS2c"), ps("pbB1c"), ps("pbB2c")
    lnbufs = (sq, t1, t2, t3, tmpf, ones_bf, ones_f, ones1f, pbS1, pbS2, pbB1, pbB2)

    def tl():
        return ("tensor", len(p.ops["tensor"]) - 1)

    if dq:
        for dc in range(8):
            l1 = p.add("sync", DMA(itile[:, :], wgT[dc * 128:(dc + 1) * 128, :]), dma=True, tag=f"lg{dc}")
            for t in range(3):
                cs = slice(t * D, (t + 1) * D)
                c1 = p.add("vector", COPY(ftile[:, :], itile[:, cs]), deps=[l1], tag=f"cg{dc}{t}")
                p.add("vector", TS1(wgb[dc][:, cs], ftile[:, :], 1.0 / 127.0, ALU.mult), deps=[c1], tag=f"sg{dc}{t}")
            p.add("sync", DMA(wg_dram[dc * 128:(dc + 1) * 128, :], wgb[dc][:, :]), dma=True, tag=f"wgs{dc}")
            p.fence()
            l2 = p.add("sync", DMA(itile[:, :], wuT[dc * 128:(dc + 1) * 128, :]), dma=True, tag=f"lu{dc}")
            for t in range(3):
                cs = slice(t * D, (t + 1) * D)
                c1 = p.add("vector", COPY(ftile[:, :], itile[:, cs]), deps=[l2], tag=f"cu{dc}{t}")
                p.add("vector", TS1(wub[dc][:, cs], ftile[:, :], 1.0 / 127.0, ALU.mult), deps=[c1], tag=f"su{dc}{t}")
            p.add("sync", DMA(wu_dram[dc * 128:(dc + 1) * 128, :], wub[dc][:, :]), dma=True, tag=f"wus{dc}")
            p.fence()
        for hc in range(24):
            l1 = p.add("sync", DMA(itile[:, 0:D], wdT[hc * 128:(hc + 1) * 128, :]), dma=True, tag=f"ld{hc}")
            c1 = p.add("vector", COPY(ftile[:, :], itile[:, 0:D]), deps=[l1], tag=f"cd{hc}")
            c2 = p.add("vector", TS1(btmp[:, :], ftile[:, :], 1.0 / 127.0, ALU.mult), deps=[c1], tag=f"sd{hc}")
            p.add("sync", DMA(wd_dram[hc * 128:(hc + 1) * 128, :], btmp[:, :]), deps=[c2], dma=True, tag=f"wd{hc}")
            p.fence()
    else:
        for dc in range(8):
            p.add("sync", DMA(wgb[dc][:, :], wg_dram[dc * 128:(dc + 1) * 128, :]), dma=True, tag=f"rg{dc}")
            p.add("sync", DMA(wub[dc][:, :], wu_dram[dc * 128:(dc + 1) * 128, :]), dma=True, tag=f"ru{dc}")
        p.fence()

    emit_ln_stream(p, lnbufs, xg, x_in, xnb, "ln3")
    p.fence()

    for tg in range(NTG):
        s = slice(tg * TG, (tg + 1) * TG)
        hdeps = {}
        evg = eva = None
        for hc in range(24):
            for dc in range(8):
                p.add("tensor", MM(pbG[:, 0:TG], wgb[dc][:, hc * 128:(hc + 1) * 128], xnb[dc][:, s],
                                   dc == 0, dc == 7), deps=[evg], tag=f"g{tg}{hc}{dc}")
            gmm = tl()
            for dc in range(8):
                p.add("tensor", MM(pbU[:, 0:TG], wub[dc][:, hc * 128:(hc + 1) * 128], xnb[dc][:, s],
                                   dc == 0, dc == 7), deps=[eva], tag=f"u{tg}{hc}{dc}")
            umm = tl()
            evg0 = p.add("scalar", ACT(sg[:, :], pbG[:, 0:TG], AF.Silu), deps=[gmm], tag=f"sl{tg}{hc}")
            hm = p.add("vector", TT(hid[hc][:, :], sg[:, :], pbU[:, 0:TG], ALU.mult),
                       deps=[evg0, umm], tag=f"hm{tg}{hc}")
            hdeps[hc] = hm
            evg = evg0
            eva = hm
        p.fence()
        dprev = {0: None, 1: None}
        wprev = {0: None, 1: None}
        for oc in range(8):
            y = pbD[oc % 2]
            for hc in range(24):
                bi = hc % 2
                wl = p.add("sync", DMA(wdt[bi][:, :], wd_dram[hc * 128:(hc + 1) * 128, oc * 128:(oc + 1) * 128]),
                           deps=[wprev[bi]], dma=True, tag=f"dl{tg}{oc}{hc}")
                p.add("tensor", MM(y[:, 0:TG], wdt[bi][:, :], hid[hc][:, :], hc == 0, hc == 23),
                      deps=[wl, hdeps[hc], dprev[oc % 2]], tag=f"d{tg}{oc}{hc}")
                wprev[bi] = tl()
            dmm = tl()
            lx = p.add("sync", DMA(xt[:, :], x_in[oc * 128:(oc + 1) * 128, s]), dma=True, tag=f"dx{tg}{oc}")
            fo = p.add("vector", STT(xt[:, :], y[:, 0:TG], 0.5, xt[:, :], ALU.mult, ALU.add),
                       deps=[lx, dmm], tag=f"df{tg}{oc}")
            p.add("sync", DMA(x_out[oc * 128:(oc + 1) * 128, s], xt[:, :]), deps=[fo], dma=True, tag=f"ds{tg}{oc}")
            dprev[oc % 2] = fo
            p.fence()
    if own:
        p.add("sync", lambda e: e.nop(), tag="final")
        p.emit(stack)
        stack.close()
        return nc
    p.fence()
    stack.close()
    return None


def build_attn(ctx=None, uid="", xin="x_in", xout="x_out"):
    own = ctx is None
    if own:
        nc = bass.Bass("TRN2", target_bir_lowering=False)
        p = Prog(nc)
        ctx = Ctx(nc, p, {})
    nc, p = ctx.nc, ctx.p
    x_in = ctx.dram(xin, [D, TOK], F32, "ExternalInput" if xin == "x_in" else "Internal")
    WT = ctx.dram("WT", [4, D, D], F32, "ExternalInput")
    pvec = ctx.dram("pvec4", [128, 32], F32, "ExternalInput")
    bvrow = ctx.dram("bvrow", [1, D], F32, "ExternalInput")
    relT = ctx.dram("relT", [64, TW], F32, "ExternalInput")
    idm = ctx.dram("idm", [128, 128], BF16, "ExternalInput")
    x_out = ctx.dram(xout, [D, TOK], F32, "ExternalOutput" if xout == "x_out" else "Internal")
    cc4i = ctx.dram("cc4i", [D, TOK], BF16, "Internal")
    cc4o = ctx.dram("cc4o", [2 * D, TOK], BF16, "Internal")
    pp0 = ctx.dram("pp0", [8, TOK * PWS + PW], BF16, "Internal")
    pp1 = ctx.dram("pp1", [8, TOK * PWS + PW], BF16, "Internal")

    stack = contextlib.ExitStack()
    sb, ps, ones_bf, ones1f, ones_f, (sq, t1, t2, t3, tmpf) = mk_common(nc, stack, p, uid + "at")

    xg = [sb(f"xg{i}", [128, TG], F32) for i in range(8)]
    p4 = [sb(f"p4_{i}", [128, TOK], BF16) for i in range(8)]
    pv = sb("pv4", [128, 32], F32)
    bqs = sb("bqs", [128, 8], F32)
    bvf = sb("bvf", [1, D], F32)
    bvb = sb("bvb", [1, D], BF16)
    ones1b = sb("ones1b", [1, 128], BF16)
    idb = sb("idb", [128, 128], BF16)
    ttb = sb("ttb", [128, TW], BF16)
    wg4 = [sb(f"wg4_{i}", [128, 256], BF16) for i in range(8)]
    wk4 = [sb(f"wk4_{i}", [128, 256], BF16) for i in range(8)]
    wv4 = [sb(f"wv4_{i}", [128, 256], BF16) for i in range(8)]
    wot = [sb(f"wot{i}", [128, 128], BF16) for i in range(8)]
    wtmp = sb("wtmp4", [128, 256], F32)
    wtmp2 = sb("wtmp2", [128, 256], F32)
    qhg = [sb(f"qhg{i}", [128, TOK], BF16) for i in range(2)]
    khg = [sb(f"khg{i}", [128, 2 * TOK], BF16) for i in range(2)]
    vhg = [sb(f"vhg{i}", [128, 4 * 65], BF16) for i in range(32)]
    g4 = [sb(f"g4_{i}", [128, TG], BF16) for i in range(8)]
    band = [sb(f"band{i}", [128, TOK], BF16) for i in range(4)]
    psb = sb("psb", [128, PWS], BF16)
    probs = [sb(f"pr{i}", [128, TG], BF16) for i in range(2)]
    attn = [sb(f"at{i}", [128, TOK], BF16) for i in range(8)]
    rs = sb("rs", [1, TG], F32)
    ao = sb("ao", [64, TG], F32)
    xt = sb("xt4", [128, TG], F32)

    pbA, pbB = ps("pbA"), ps("pbB")
    pbS = [ps("pbSa"), ps("pbSb")]
    pbO = ps("pbO")
    pbS1, pbS2, pbB1, pbB2 = pbA, pbB, pbS[0], pbS[1]
    lnbufs = (sq, t1, t2, t3, tmpf, ones_bf, ones_f, ones1f, pbS1, pbS2, pbB1, pbB2)

    def tl():
        return ("tensor", len(p.ops["tensor"]) - 1)

    p.add("vector", MEMSET(ones1b[:, :], 1.0), tag="ones1b")
    lpv = p.add("sync", DMA(pv[:, :], pvec[:, :]), dma=True, tag="lpv")
    p.add("vector", TS1(bqs[:, :], pv[:, 0:8], 0.125, ALU.mult), deps=[lpv], tag="bqs")
    lbv = p.add("sync", DMA(bvf[:, :], bvrow[:, :]), dma=True, tag="lbv")
    p.add("vector", COPY(bvb[:, :], bvf[:, :]), deps=[lbv], tag="cbv")
    lid = p.add("sync", DMA(idb[:, :], idm[:, :]), dma=True, tag="lid")
    prevc = None
    for t in range(24):
        w = min(256, TW - t * 256)
        lt_ = p.add("sync", DMA(wtmp2[0:64, 0:w], relT[:, t * 256:t * 256 + w]), deps=[prevc], dma=True, tag=f"ltt{t}")
        c1_ = p.add("vector", COPY(ttb[0:64, t * 256:t * 256 + w], wtmp2[0:64, 0:w]), deps=[lt_], tag=f"ctt{t}")
        prevc = p.add("vector", COPY(ttb[64:128, t * 256:t * 256 + w], wtmp2[0:64, 0:w]), deps=[lt_, c1_], tag=f"ct2{t}")
    p.fence()

    emit_ln_stream(p, lnbufs, xg, x_in, p4, "ln4")
    for dc in range(8):
        p.add("sync", DMA(cc4i[dc * 128:(dc + 1) * 128, :], p4[dc][:, :]), dma=True, tag=f"sp4{dc}")
    p.fence()
    cc = p.add("gpsimd", lambda e: e.collective_compute(
        "AllGather", ALU.bypass, replica_groups=[[0, 1], [2, 3], [4, 5], [6, 7]],
        ins=[cc4i[:, :]], outs=[cc4o[:, :]]), tag="cc4")
    p.fence()

    for g in range(4):
        fs = slice(g * 256, (g + 1) * 256)
        for ec in range(8):
            l1 = p.add("sync", DMA(wtmp[:, :], WT[0, ec * 128:(ec + 1) * 128, fs]), dma=True, tag=f"lwq{g}{ec}")
            c1 = p.add("vector", COPY(wg4[ec][:, :], wtmp[:, :]), deps=[l1], tag=f"cwq{g}{ec}")
            l2 = p.add("sync", DMA(wtmp[:, :], WT[1, ec * 128:(ec + 1) * 128, fs]), deps=[c1], dma=True, tag=f"lwk{g}{ec}")
            c2 = p.add("vector", COPY(wk4[ec][:, :], wtmp[:, :]), deps=[l2], tag=f"cwk{g}{ec}")
            l3 = p.add("sync", DMA(wtmp[:, :], WT[2, ec * 128:(ec + 1) * 128, fs]), deps=[c2], dma=True, tag=f"lwv{g}{ec}")
            p.add("vector", COPY(wv4[ec][:, :], wtmp[:, :]), deps=[l3], tag=f"cwv{g}{ec}")
            p.fence()
        for fc in range(2):
            for tg in range(NTG):
                s = slice(tg * TG, (tg + 1) * TG)
                for ec in range(8):
                    p.add("tensor", MM(pbA[:, 0:TG], wg4[ec][:, fc * 128:(fc + 1) * 128], p4[ec][:, s],
                                       ec == 0, ec == 7), tag=f"q{g}{fc}{tg}{ec}")
                p.add("scalar", ACT(qhg[fc][:, s], pbA[:, 0:TG], AF.Identity,
                                    bias=bqs[:, g * 2 + fc:g * 2 + fc + 1], scale=0.125),
                      deps=[tl()], tag=f"eq{g}{fc}{tg}")
                p.fence()
        for jg in range(8):
            half, col = jg // 4, (jg % 4) * TG
            scol_ = slice(col, col + TG)
            for ec in range(8):
                p.add("sync", DMA(g4[ec][:, :], cc4o[half * D + ec * 128:half * D + (ec + 1) * 128, scol_]),
                      deps=[cc], dma=True, tag=f"lg4{g}{jg}{ec}")
            p.fence()
            for fc in range(2):
                for ec in range(8):
                    p.add("tensor", MM(pbA[:, 0:TG], wk4[ec][:, fc * 128:(fc + 1) * 128], g4[ec][:, :],
                                       ec == 0, ec == 7), tag=f"k{g}{jg}{fc}{ec}")
                p.add("scalar", ACT(khg[fc][:, jg * TG:(jg + 1) * TG], pbA[:, 0:TG], AF.Identity,
                                    bias=pv[:, 8 + g * 2 + fc:8 + g * 2 + fc + 1]), deps=[tl()], tag=f"ek{g}{jg}{fc}")
                p.fence()
            for sub in range(4):
                jt = jg * 4 + sub
                for ec in range(8):
                    p.add("tensor", MM(pbB[:, 0:256], g4[ec][:, sub * 128:(sub + 1) * 128], wv4[ec][:, :],
                                       ec == 0, False), tag=f"v{g}{jt}{ec}")
                p.add("tensor", MM(pbB[:, 0:256], ones1b[:, :], bvb[0:1, fs], False, True), tag=f"vb{g}{jt}")
                vbm = tl()
                ms = p.add("vector", MEMSET(vhg[jt].rearrange("p (h u) -> p h u", u=65)[:, :, 64:65], 1.0),
                           tag=f"vo{g}{jt}")
                p.add("scalar", (lambda jt=jt: lambda e: e.copy(
                    vhg[jt].rearrange("p (h u) -> p h u", u=65)[:, :, 0:64],
                    pbB[:, 0:256].rearrange("p (h u) -> p h u", u=64)))(),
                    deps=[vbm, ms], tag=f"ev{g}{jt}")
                p.fence()
        for hh in range(4):
            hglob = g * 4 + hh
            qsl = qhg[hh // 2][64 * (hh % 2):64 * (hh % 2) + 64, :]
            for ib in range(16):
                c0 = 1920 - 128 * ib
                evd = []
                tsl = ttb[64 * (hh % 2):64 * (hh % 2) + 64, :]
                for ch in range(9):
                    wch = min(512, PW - ch * 512)
                    mm = p.add("tensor", MM(pbS[ch % 2][0:128, 0:wch], qsl[:, ib * 128:(ib + 1) * 128],
                                            tsl[:, c0 + ch * 512:c0 + ch * 512 + wch], True, True),
                               deps=[evd[ch - 2]] if ch >= 2 else [], tag=f"pp{hglob}{ib}{ch}")
                    ev = p.add("vector", COPY(psb[:, ch * 512:ch * 512 + wch], pbS[ch % 2][0:128, 0:wch]),
                               deps=[mm], tag=f"pe{hglob}{ib}{ch}")
                    evd.append(ev)
                ppt = pp0 if hglob < 8 else pp1
                p.add("sync", DMA(ppt[hglob % 8, ib * 128 * PWS:(ib + 1) * 128 * PWS]
                                  .rearrange("(r u) -> r u", u=PWS)[:, :], psb[:, :]),
                      deps=evd, dma=True, tag=f"pw{hglob}{ib}")
                p.fence()
        for hh in range(4):
            hglob = g * 4 + hh
            fc, pb_ = hh // 2, 64 * (hh % 2)
            achunk, abase = hglob // 2, 64 * (hglob % 2)
            ksl = khg[fc][pb_:pb_ + 64, :]
            qsl = qhg[fc][pb_:pb_ + 64, :]
            for ig in range(4):
                pvd = None
                exh = {}
                for jh in range(2):
                    bd = []
                    for ibl in range(4):
                        ib = ig * 4 + ibl
                        ppt = pp0 if hglob < 8 else pp1
                        flat = ppt[hglob % 8, ib * 128 * PWS + 127 + jh * TOK:
                                   ib * 128 * PWS + 127 + jh * TOK + 128 * PW]
                        sk = flat.rearrange("(r j) -> r j", j=PW)[:, 0:TOK]
                        bd.append(p.add("sync", DMA(band[ibl][:, 0:TOK], sk),
                                        deps=[pvd, exh.get(16 * jh - 1), exh.get(16 * jh - 2)],
                                        dma=True, tag=f"bd{hglob}{ig}{jh}{ibl}"))
                    for jt in range(16 * jh, 16 * jh + 16):
                        jl = jt - 16 * jh
                        y = pbS[jt % 2]
                        p.add("tensor", MM(y[:, 0:TG], ksl[:, jt * 128:(jt + 1) * 128], qsl[:, ig * TG:(ig + 1) * TG],
                                           True, False), deps=[pvd, exh.get(jt - 2)], tag=f"s{hglob}{ig}{jt}")
                        for ibl in range(4):
                            p.add("tensor", MM(y[:, ibl * 128:(ibl + 1) * 128], band[ibl][:, jl * 128:(jl + 1) * 128],
                                               idb[:, :], False, ibl == 3), deps=[bd[ibl], lid], tag=f"i{hglob}{ig}{jt}{ibl}")
                        ex = p.add("scalar", ACT(probs[jt % 2][:, :], y[:, 0:TG], AF.Exp),
                                   deps=[tl()], tag=f"x{hglob}{ig}{jt}")
                        exh[jt] = ex
                        pvd = p.add("tensor", MM(pbO[0:65, 0:TG], vhg[jt][:, hh * 65:(hh + 1) * 65],
                                                 probs[jt % 2][:, :], jt == 0, jt == 31), deps=[ex], tag=f"pv{hglob}{ig}{jt}")
                c1 = p.add("scalar", SCOPY(rs[0:1, :], pbO[64:65, 0:TG]), deps=[pvd], tag=f"nr{hglob}{ig}")
                c2 = p.add("vector", (lambda: lambda e: e.reciprocal(rs[0:1, :], rs[0:1, :]))(),
                           deps=[c1], tag=f"nv{hglob}{ig}")
                c3 = p.add("tensor", MM(pbB[0:64, 0:TG], ones1f[0:1, 0:64], rs[0:1, :], True, True),
                           deps=[c2], tag=f"nb{hglob}{ig}")
                c4 = p.add("scalar", SCOPY(ao[:, :], pbO[0:64, 0:TG]), deps=[pvd], tag=f"na{hglob}{ig}")
                c5 = p.add("vector", TT(attn[achunk][abase:abase + 64, ig * TG:(ig + 1) * TG], ao[:, :],
                                        pbB[0:64, 0:TG], ALU.mult), deps=[c3, c4], tag=f"nm{hglob}{ig}")
                p.fence()
    for fc in range(8):
        for ec in range(8):
            lw = p.add("sync", DMA(wtmp2[:, 0:128], WT[3, ec * 128:(ec + 1) * 128, fc * 128:(fc + 1) * 128]),
                       dma=True, tag=f"lwo{fc}{ec}")
            p.add("vector", COPY(wot[ec][:, :], wtmp2[:, 0:128]), deps=[lw], tag=f"cwo{fc}{ec}")
            p.fence()
        for tg in range(NTG):
            s = slice(tg * TG, (tg + 1) * TG)
            for ec in range(8):
                p.add("tensor", MM(pbA[:, 0:TG], wot[ec][:, :], attn[ec][:, s], ec == 0, ec == 7),
                      tag=f"o{tg}{fc}{ec}")
            eo = p.add("scalar", ACT(tmpf[:, :], pbA[:, 0:TG], AF.Identity, bias=pv[:, 24 + fc:24 + fc + 1]),
                       deps=[tl()], tag=f"eo{tg}{fc}")
            lx = p.add("sync", DMA(xt[:, :], x_in[fc * 128:(fc + 1) * 128, s]), dma=True, tag=f"ox{tg}{fc}")
            fo = p.add("vector", TT(xt[:, :], xt[:, :], tmpf[:, :], ALU.add), deps=[eo, lx], tag=f"of{tg}{fc}")
            p.add("sync", DMA(x_out[fc * 128:(fc + 1) * 128, s], xt[:, :]), deps=[fo], dma=True, tag=f"os{tg}{fc}")
            p.fence()
    if own:
        p.add("sync", lambda e: e.nop(), tag="final")
        p.emit(stack)
        stack.close()
        return nc
    p.fence()
    stack.close()
    return None


def build_mega():
    nc = bass.Bass("TRN2", target_bir_lowering=False)
    p = Prog(nc, strict=True)
    ctx = Ctx(nc, p, {})
    build_ffn0(ctx, uid="a", xin="x_in", xout="xs1", dq=True)
    build_scan(ctx, uid="b", xin="xs1", xout="xs2")
    build_ffn0(ctx, uid="c", xin="xs2", xout="xs3", dq=False)
    build_ffn1(ctx, uid="d", xin="xs3", xout="xs4", dq=True)
    p.strict = False
    build_attn(ctx, uid="e", xin="xs4", xout="xs5")
    p.strict = True
    build_ffn1(ctx, uid="f", xin="xs5", xout="x_out", dq=False)
    p.fence()
    p.add("sync", lambda e: e.nop(), tag="final")
    st = contextlib.ExitStack()
    p.emit(st)
    st.close()
    return nc


_NC_CACHE = {}


def _get(name, builder):
    if name not in _NC_CACHE:
        _NC_CACHE[name] = builder()
    return _NC_CACHE[name]


def _launch(nc, in_maps):
    res = run_bass_kernel_spmd(nc, in_maps, core_ids=list(range(8)))
    return res.results


def kernel(L0_attn_x, L0_attn_a_param, L0_attn_w_in, L0_attn_b_in, L0_attn_w_a, L0_attn_b_a,
           L0_ffn_qweights, L0_ffn_scales, L0_ffn_biases, L0_ffn_lora_a, L0_ffn_lora_b,
           L0_ffn_ln_gamma, L0_ffn_ln_beta,
           L1_attn_Wq, L1_attn_bq, L1_attn_Wk, L1_attn_bk, L1_attn_Wv, L1_attn_bv,
           L1_attn_Wo, L1_attn_bo, L1_attn_rel_table,
           L1_ffn_w_gate, L1_ffn_w_up, L1_ffn_w_down):
    x = np.ascontiguousarray(np.asarray(L0_attn_x, dtype=np.float32))

    def shards(xfull):
        out = []
        for c in range(8):
            b, hhalf = c // 2, c % 2
            out.append(np.ascontiguousarray(xfull[b, hhalf * TOK:(hhalf + 1) * TOK, :].T))
        return out

    xs = shards(x)

    qT = np.ascontiguousarray(np.asarray(L0_ffn_qweights).transpose(0, 2, 1)).astype(np.int32)
    sr = np.repeat(np.asarray(L0_ffn_scales, dtype=np.float32), 16, axis=2)
    srT = np.ascontiguousarray(sr.transpose(0, 2, 1))
    laT = np.ascontiguousarray(np.asarray(L0_ffn_lora_a, dtype=np.float32).transpose(0, 2, 1))
    lbT = np.ascontiguousarray(np.asarray(L0_ffn_lora_b, dtype=np.float32).transpose(0, 2, 1))
    pv1 = np.zeros((128, 96), np.float32)
    pv1[:, 0:48] = np.asarray(L0_ffn_biases, np.float32).reshape(6, 8, 128).transpose(2, 0, 1).reshape(128, 48)
    pv1[:, 48:72] = np.asarray(L0_ffn_ln_gamma, np.float32).reshape(3, 8, 128).transpose(2, 0, 1).reshape(128, 24)
    pv1[:, 72:96] = np.asarray(L0_ffn_ln_beta, np.float32).reshape(3, 8, 128).transpose(2, 0, 1).reshape(128, 24)

    nc1 = _get("mega", build_mega)
    pv2 = np.zeros((128, 24), np.float32)
    pv2[:, 0:8] = np.asarray(L0_attn_b_in, np.float32).T
    pv2[:, 8:16] = np.asarray(L0_attn_b_a, np.float32).T
    pv2[:, 16:24] = np.asarray(L0_attn_a_param, np.float32).reshape(8, 128).T
    wgT = np.ascontiguousarray(np.asarray(L1_ffn_w_gate).T).astype(np.int32)
    wuT = np.ascontiguousarray(np.asarray(L1_ffn_w_up).T).astype(np.int32)
    wdT = np.ascontiguousarray(np.asarray(L1_ffn_w_down).T).astype(np.int32)
    WTs = np.stack([np.ascontiguousarray(np.asarray(w, np.float32).T) for w in
                    (L1_attn_Wq, L1_attn_Wk, L1_attn_Wv, L1_attn_Wo)])
    pv4 = np.zeros((128, 32), np.float32)
    for i, b in enumerate((L1_attn_bq, L1_attn_bk, L1_attn_bv, L1_attn_bo)):
        pv4[:, i * 8:(i + 1) * 8] = np.asarray(b, np.float32).reshape(8, 128).T
    rel = np.asarray(L1_attn_rel_table, np.float32)
    maps = []
    for c in range(8):
        sel = np.zeros((1, 1), np.float32)
        if c % 2 == 1:
            sel[0, 0] = 1.0
        off = (c % 2) * TOK
        idx = np.clip(off + 3070 - np.arange(TW), 0, 2046)
        relT = np.ascontiguousarray(rel[idx].T)
        maps.append(dict(x_in=xs[c], qT=qT, srT=srT, laT=laT, lbT=lbT, pvec=pv1,
                         w_in=np.asarray(L0_attn_w_in, np.float32),
                         w_a=np.asarray(L0_attn_w_a, np.float32), pvec2=pv2, selm=sel,
                         wgT=wgT, wuT=wuT, wdT=wdT, WT=WTs, pvec4=pv4,
                         bvrow=np.asarray(L1_attn_bv, np.float32).reshape(1, D),
                         relT=relT, idm=np.eye(128).astype(BF)))
    r = _launch(nc1, maps)
    x6 = [np.asarray(r[c]["x_out"]) for c in range(8)]

    out = np.empty((4, 4096, D), np.float32)
    for c in range(8):
        b, hhalf = c // 2, c % 2
        out[b, hhalf * TOK:(hhalf + 1) * TOK, :] = x6[c].T
    return out


# version 05 (final)
# speedup vs baseline: 1.0004x; 1.0004x over previous
import contextlib
import numpy as np
import ml_dtypes
import concourse.bass as bass
import concourse.mybir as mybir
from concourse.bass_utils import run_bass_kernel_spmd

F32 = mybir.dt.float32
BF16 = mybir.dt.bfloat16
I32 = mybir.dt.int32
AF = mybir.ActivationFunctionType
ALU = mybir.AluOpType
BF = ml_dtypes.bfloat16

ENGINES = ("tensor", "vector", "scalar", "gpsimd", "sync")
D = 1024
TOK = 2048
TG = 512
NTG = 4
LOG1M = float(np.log1p(-1e-5))
TW = 6143
PW = 4223
PWS = 4224


class Prog:
    def __init__(self, nc, strict=False):
        self.nc = nc
        self.strict = strict
        self.ops = {e: [] for e in ENGINES}
        self.global_deps = []

    def add(self, eng, fn, deps=(), dma=False, tag=""):
        if self.strict:
            self.fence()
        deps = [d for d in deps if d is not None] + list(self.global_deps)
        i = len(self.ops[eng])
        self.ops[eng].append((fn, deps, dma, tag))
        return (eng, i)

    def fence(self):
        self.global_deps = [(e, len(self.ops[e]) - 1) for e in ENGINES if self.ops[e]]

    def emit(self, stack):
        nc = self.nc
        sems = {e: stack.enter_context(nc.semaphore(name=f"sem_{e}")) for e in ENGINES}
        awaited = set()
        for e in ENGINES:
            for i, (fn, deps, dma, tag) in enumerate(self.ops[e]):
                for (pe, pi) in deps:
                    if pe == e and pi >= i:
                        continue
                    if pe == "tensor" and e == "tensor":
                        continue
                    awaited.add((pe, pi))
        mile = {}
        counts = {e: 0 for e in ENGINES}
        for e in ENGINES:
            for i in range(len(self.ops[e])):
                if (e, i) in awaited or self.ops[e][i][2]:
                    counts[e] += 16 if self.ops[e][i][2] else 1
                    mile[(e, i)] = counts[e]

        def mk_stream(e):
            def stream(eng):
                waited = {}
                for i, (fn, deps, dma, tag) in enumerate(self.ops[e]):
                    for (pe, pi) in deps:
                        if pe == e and pi >= i:
                            continue
                        if pe == "tensor" and e == "tensor":
                            continue
                        v = mile.get((pe, pi))
                        if v is None:
                            raise RuntimeError(f"dep {(pe,pi)} not milestoned; consumer {(e,i)}:{tag}")
                        if waited.get(pe, 0) < v:
                            eng.wait_ge(sems[pe], v)
                            waited[pe] = v
                    ins = fn(eng)
                    if (e, i) in mile:
                        ins.then_inc(sems[e], 16 if dma else 1)
            return stream

        with nc.Block() as block:
            for e in ENGINES:
                if self.ops[e]:
                    getattr(block, e)(mk_stream(e))


def MM(out, lhsT, rhs, start, stop):
    return lambda e: e.matmul(out, lhsT, rhs, start=start, stop=stop)


def ACT(out, in_, func, bias=0.0, scale=1.0):
    return lambda e: e.activation(out, in_, func, bias=bias, scale=scale)


def TT(out, a, b, op):
    return lambda e: e.tensor_tensor(out, a, b, op)


def TS(out, in_, s1, s2, op0, op1):
    return lambda e: e.tensor_scalar(out, in_, s1, s2, op0, op1)


def TS1(out, in_, s, op):
    return lambda e: e.tensor_single_scalar(out, in_, s, op)


def STT(out, in0, s, in1, op0, op1):
    return lambda e: e.scalar_tensor_tensor(out, in0, s, in1, op0, op1)


def SCAN(out, d0, d1, init, op0, op1):
    return lambda e: e.tensor_tensor_scan(out, d0, d1, init, op0, op1)


def DMA(out, in_):
    return lambda e: e.dma_start(out=out, in_=in_)


def COPY(out, in_):
    return lambda e: e.tensor_copy(out, in_)


def SCOPY(out, in_):
    return lambda e: e.copy(out, in_)


def MEMSET(ap, v):
    return lambda e: e.memset(ap, v)


def emit_ln(p, bufs, src, srcdeps, dst, name, dst_bf16_tmp=None):
    sq, t1, t2, t3, tmpf, ones_bf, ones_f, ones1f, pbS1, pbS2, pbB1, pbB2 = bufs
    outd = {dc: [] for dc in range(8)}
    for tg in range(NTG):
        s = slice(tg * TG, (tg + 1) * TG)
        mm_s = mm_q = qprev = None
        for dc in range(8):
            q = p.add("scalar", ACT(sq[:, :], src[dc][:, s], AF.Square),
                      deps=srcdeps[dc] + [mm_q], tag=f"{name}q{tg}{dc}")
            mm_s = p.add("tensor", MM(pbS1[0:1, 0:TG], ones_f[:, 0:1], src[dc][:, s], dc == 0, dc == 7),
                         deps=srcdeps[dc], tag=f"{name}ms{tg}{dc}")
            mm_q = p.add("tensor", MM(pbS2[0:1, 0:TG], ones_f[:, 0:1], sq[:, :], dc == 0, dc == 7),
                         deps=[q], tag=f"{name}mq{tg}{dc}")
        a1 = p.add("vector", TS1(t1[:, :], pbS1[0:1, 0:TG], 1.0 / D, ALU.mult), deps=[mm_s], tag=f"{name}a1{tg}")
        a2 = p.add("vector", TS1(t2[:, :], pbS2[0:1, 0:TG], 1.0 / D, ALU.mult), deps=[mm_q], tag=f"{name}a2{tg}")
        a3 = p.add("vector", TT(t3[:, :], t1[:, :], t1[:, :], ALU.mult), deps=[a1], tag=f"{name}a3{tg}")
        a4 = p.add("vector", TT(t2[:, :], t2[:, :], t3[:, :], ALU.subtract), deps=[a2, a3], tag=f"{name}a4{tg}")
        r1 = p.add("scalar", ACT(t2[:, :], t2[:, :], AF.Ln, bias=1e-5), deps=[a4], tag=f"{name}r1{tg}")
        r2 = p.add("scalar", ACT(t2[:, :], t2[:, :], AF.Exp, scale=-0.5), deps=[r1], tag=f"{name}r2{tg}")
        nm = p.add("vector", STT(t3[:, :], t1[:, :], -1.0, t2[:, :], ALU.mult, ALU.mult),
                   deps=[a3, r2], tag=f"{name}nm{tg}")
        b1 = p.add("tensor", MM(pbB1[:, 0:TG], ones1f[0:1, :], t2[0:1, :], True, True), deps=[r2], tag=f"{name}b1{tg}")
        b2 = p.add("tensor", MM(pbB2[:, 0:TG], ones1f[0:1, :], t3[0:1, :], True, True), deps=[nm], tag=f"{name}b2{tg}")
        prev = None
        for dc in range(8):
            w1 = p.add("vector", TT(tmpf[:, :], src[dc][:, s], pbB1[:, 0:TG], ALU.mult),
                       deps=srcdeps[dc] + [b1, prev], tag=f"{name}w1{tg}{dc}")
            w2 = p.add("vector", TT(dst[dc][:, s], tmpf[:, :], pbB2[:, 0:TG], ALU.add),
                       deps=[w1, b2], tag=f"{name}w2{tg}{dc}")
            outd[dc].append(w2)
            prev = w2
        p.fence()
    return outd


def emit_ln_stream(p, bufs, xg, x_dram, dst, name):
    sq, t1, t2, t3, tmpf, ones_bf, ones_f, ones1f, pbS1, pbS2, pbB1, pbB2 = bufs
    outd = {dc: [] for dc in range(8)}
    for tg in range(NTG):
        s = slice(tg * TG, (tg + 1) * TG)
        ldd = {dc: [p.add("sync", DMA(xg[dc][:, :], x_dram[dc * 128:(dc + 1) * 128, s]),
                          dma=True, tag=f"{name}ld{tg}{dc}")] for dc in range(8)}
        mm_s = mm_q = None
        for dc in range(8):
            q = p.add("scalar", ACT(sq[:, :], xg[dc][:, :], AF.Square),
                      deps=ldd[dc] + [mm_q], tag=f"{name}q{tg}{dc}")
            mm_s = p.add("tensor", MM(pbS1[0:1, 0:TG], ones_f[:, 0:1], xg[dc][:, :], dc == 0, dc == 7),
                         deps=ldd[dc], tag=f"{name}ms{tg}{dc}")
            mm_q = p.add("tensor", MM(pbS2[0:1, 0:TG], ones_f[:, 0:1], sq[:, :], dc == 0, dc == 7),
                         deps=[q], tag=f"{name}mq{tg}{dc}")
        a1 = p.add("vector", TS1(t1[:, :], pbS1[0:1, 0:TG], 1.0 / D, ALU.mult), deps=[mm_s], tag=f"{name}a1{tg}")
        a2 = p.add("vector", TS1(t2[:, :], pbS2[0:1, 0:TG], 1.0 / D, ALU.mult), deps=[mm_q], tag=f"{name}a2{tg}")
        a3 = p.add("vector", TT(t3[:, :], t1[:, :], t1[:, :], ALU.mult), deps=[a1], tag=f"{name}a3{tg}")
        a4 = p.add("vector", TT(t2[:, :], t2[:, :], t3[:, :], ALU.subtract), deps=[a2, a3], tag=f"{name}a4{tg}")
        r1 = p.add("scalar", ACT(t2[:, :], t2[:, :], AF.Ln, bias=1e-5), deps=[a4], tag=f"{name}r1{tg}")
        r2 = p.add("scalar", ACT(t2[:, :], t2[:, :], AF.Exp, scale=-0.5), deps=[r1], tag=f"{name}r2{tg}")
        nm = p.add("vector", STT(t3[:, :], t1[:, :], -1.0, t2[:, :], ALU.mult, ALU.mult),
                   deps=[a3, r2], tag=f"{name}nm{tg}")
        b1 = p.add("tensor", MM(pbB1[:, 0:TG], ones1f[0:1, :], t2[0:1, :], True, True), deps=[r2], tag=f"{name}b1{tg}")
        b2 = p.add("tensor", MM(pbB2[:, 0:TG], ones1f[0:1, :], t3[0:1, :], True, True), deps=[nm], tag=f"{name}b2{tg}")
        prev = None
        for dc in range(8):
            w1 = p.add("vector", TT(tmpf[:, :], xg[dc][:, :], pbB1[:, 0:TG], ALU.mult),
                       deps=ldd[dc] + [b1, prev], tag=f"{name}w1{tg}{dc}")
            w2 = p.add("vector", TT(dst[dc][:, s], tmpf[:, :], pbB2[:, 0:TG], ALU.add),
                       deps=[w1, b2], tag=f"{name}w2{tg}{dc}")
            outd[dc].append(w2)
            prev = w2
        p.fence()
    return outd


def mk_common(nc, stack, p, uid=""):
    def sb(name, shape, dt):
        return stack.enter_context(nc.sbuf_tensor(uid + name, shape, dt)).ap()

    def ps(n):
        return stack.enter_context(nc.psum_tensor(uid + n, [128, 512], F32)).ap()

    ones_bf = sb("c_ones_bf", [128, 1], BF16)
    ones_f = sb("c_ones_f", [128, 1], F32)
    ones1f = sb("c_ones1f", [1, 128], F32)
    sq = sb("c_sq", [128, TG], F32)
    t1 = sb("c_t1", [1, TG], F32)
    t2 = sb("c_t2", [1, TG], F32)
    t3 = sb("c_t3", [1, TG], F32)
    tmpf = sb("c_tmpf", [128, TG], F32)
    p.add("vector", MEMSET(ones_bf[:, :], 1.0), tag="c_ones")
    p.add("vector", MEMSET(ones_f[:, :], 1.0), tag="c_onesf")
    p.add("vector", MEMSET(ones1f[:, :], 1.0), tag="c_ones1")
    cst = sb("c_cst", [128, 2], F32)
    p.add("vector", MEMSET(cst[:, 0:1], 1e-5), tag="c_c0")
    p.add("vector", MEMSET(cst[:, 1:2], 1.0), tag="c_c1")
    if not getattr(nc, "_cst_done", False):
        nc.const_aps.register(F32, 1e-5, cst[:, 0:1])
        nc.const_aps.register(F32, 1.0, cst[:, 1:2])
        nc._cst_done = True
    p.fence()
    return sb, ps, ones_bf, ones1f, ones_f, (sq, t1, t2, t3, tmpf)


class Ctx:
    def __init__(self, nc, p, dram_cache):
        self.nc, self.p, self.dcache = nc, p, dram_cache

    def dram(self, name, shape, dt, kind):
        if name not in self.dcache:
            self.dcache[name] = self.nc.dram_tensor(name, shape, dt, kind=kind).ap()
        return self.dcache[name]


def build_ffn0(ctx=None, uid="", xin="x_in", xout="x_out", dq=True):
    own = ctx is None
    if own:
        nc = bass.Bass("TRN2", target_bir_lowering=False)
        p = Prog(nc, strict=True)
        ctx = Ctx(nc, p, {})
    nc, p = ctx.nc, ctx.p
    x_in = ctx.dram(xin, [D, TOK], F32, "ExternalInput" if xin == "x_in" else "Internal")
    qT = ctx.dram("qT", [6, D, D], I32, "ExternalInput")
    srT = ctx.dram("srT", [6, D, D], F32, "ExternalInput")
    laT = ctx.dram("laT", [6, D, 32], F32, "ExternalInput")
    lbT = ctx.dram("lbT", [6, 32, D], F32, "ExternalInput")
    pvec = ctx.dram("pvec", [128, 96], F32, "ExternalInput")
    x_out = ctx.dram(xout, [D, TOK], F32, "ExternalOutput" if xout == "x_out" else "Internal")
    wdram = ctx.dram("p1_wdram", [6, D, D], BF16, "Internal")

    stack = contextlib.ExitStack()
    sb, ps, ones_bf, ones1f, ones_f, (sq, t1, t2, t3, tmpf) = mk_common(nc, stack, p, uid + "f0")

    h = [sb(f"h{i}", [128, TOK], F32) for i in range(8)]
    hb = [sb(f"hb{i}", [128, TG], BF16) for i in range(8)]
    gb = [sb(f"gb{i}", [128, TG], BF16) for i in range(8)]
    y1b = [sb(f"y1b{i}", [128, TG], F32) for i in range(8)]
    wcur = [sb(f"wc{i}", [128, 2 * D], BF16) for i in range(8)]
    la_sb = sb("la_sb", [128, 6 * 8 * 32], BF16)
    lb_sb = sb("lb_sb", [32, 6 * D], BF16)
    pv = sb("pv", [128, 96], F32)
    xtile = sb("xtile", [128, TG], F32)
    qtile = [sb(f"qt{i}", [128, D], I32) for i in range(2)]
    srtile = [sb(f"st{i}", [128, D], F32) for i in range(2)]
    wtmp = sb("wtmp", [128, D], F32)
    wq_bf = [sb(f"wqb{i}", [128, D], BF16) for i in range(2)]
    rb = sb("rb", [32, TG], BF16)
    ftmp = sb("ftmp", [128, TG], F32)
    ftmp2 = sb("ftmp2", [128, TG], F32)
    ls32 = sb("ls32", [128, 32], F32)
    lbtmp = sb("lbtmp", [32, D], F32)

    pbY = [ps("pbY0"), ps("pbY1")]
    pbR = ps("pbR")
    pbS1, pbS2, pbB1, pbB2 = ps("pbS1"), ps("pbS2"), ps("pbB1"), ps("pbB2")
    lnbufs = (sq, t1, t2, t3, tmpf, ones_bf, ones_f, ones1f, pbS1, pbS2, pbB1, pbB2)

    p.add("sync", DMA(pv[:, :], pvec[:, :]), dma=True, tag="ldpv")
    for m in range(6):
        for dc in range(8):
            l1 = p.add("sync", DMA(ls32[:, :], laT[m, dc * 128:(dc + 1) * 128, :]), dma=True, tag=f"lla{m}{dc}")
            p.add("vector", COPY(la_sb[:, (m * 8 + dc) * 32:(m * 8 + dc + 1) * 32], ls32[:, :]),
                  deps=[l1], tag=f"cla{m}{dc}")
            p.fence()
        l2 = p.add("sync", DMA(lbtmp[:, :], lbT[m, :, :]), dma=True, tag=f"llb{m}")
        p.add("vector", COPY(lb_sb[:, m * D:(m + 1) * D], lbtmp[:, :]), deps=[l2], tag=f"clb{m}")
        p.fence()

    for m in range(6 if dq else 0):
        for dc in range(8):
            bi = dc % 2
            lq = p.add("sync", DMA(qtile[bi][:, :], qT[m, dc * 128:(dc + 1) * 128, :]), dma=True, tag=f"lq{m}{dc}")
            lsr = p.add("sync", DMA(srtile[bi][:, :], srT[m, dc * 128:(dc + 1) * 128, :]), dma=True, tag=f"lsr{m}{dc}")
            cc = p.add("vector", COPY(wtmp[:, :], qtile[bi][:, :]), deps=[lq], tag=f"dc{m}{dc}")
            ca = p.add("vector", TS(wtmp[:, :], wtmp[:, :], 1.0 / 7.5, -1.0, ALU.mult, ALU.add),
                       deps=[cc], tag=f"da{m}{dc}")
            cm = p.add("vector", TT(wq_bf[bi][:, :], wtmp[:, :], srtile[bi][:, :], ALU.mult),
                       deps=[ca, lsr], tag=f"dm{m}{dc}")
            p.add("sync", DMA(wdram[m, dc * 128:(dc + 1) * 128, :], wq_bf[bi][:, :]),
                  deps=[cm], dma=True, tag=f"dw{m}{dc}")
            if bi == 1:
                p.fence()
    p.fence()

    xdeps = {dc: [p.add("sync", DMA(h[dc][:, :], x_in[dc * 128:(dc + 1) * 128, :]), dma=True, tag=f"lx{dc}")]
             for dc in range(8)}
    p.fence()
    emit_ln(p, lnbufs, h, xdeps, h, "ln0")
    p.fence()

    h_w = {dc: [] for dc in range(8)}
    for blk in range(3):
        m1, m2 = 2 * blk, 2 * blk + 1
        wl = {}
        for dc in range(8):
            wl[dc] = [
                p.add("sync", DMA(wcur[dc][:, 0:D], wdram[m1, dc * 128:(dc + 1) * 128, :]), dma=True, tag=f"wA{blk}{dc}"),
                p.add("sync", DMA(wcur[dc][:, D:2 * D], wdram[m2, dc * 128:(dc + 1) * 128, :]), dma=True, tag=f"wB{blk}{dc}"),
            ]
        p.fence()
        for tg in range(NTG):
            s = slice(tg * TG, (tg + 1) * TG)
            cb = {dc: p.add("scalar", ACT(hb[dc][:, :], h[dc][:, s], AF.Copy), deps=h_w[dc], tag=f"cb{blk}{tg}{dc}")
                  for dc in range(8)}
            mm = None
            for dc in range(8):
                mm = p.add("tensor", MM(pbR[0:32, 0:TG], la_sb[:, (m1 * 8 + dc) * 32:(m1 * 8 + dc + 1) * 32],
                                        hb[dc][:, :], dc == 0, dc == 7), deps=[cb[dc]], tag=f"r1{blk}{tg}{dc}")
            er = p.add("scalar", ACT(rb[:, :], pbR[0:32, 0:TG], AF.Copy), deps=[mm], tag=f"er1{blk}{tg}")
            ev_prev = {0: None, 1: None}
            for oc in range(8):
                y = pbY[oc % 2]
                for dc in range(8):
                    p.add("tensor", MM(y[:, 0:TG], wcur[dc][:, oc * 128:(oc + 1) * 128],
                                       hb[dc][:, :], dc == 0, False),
                          deps=[cb[dc], wl[dc][0], ev_prev[oc % 2]], tag=f"y1{blk}{tg}{oc}{dc}")
                mmy = p.add("tensor", MM(y[:, 0:TG], lb_sb[:, m1 * D + oc * 128:m1 * D + (oc + 1) * 128],
                                         rb[:, :], False, True), deps=[er], tag=f"y1l{blk}{tg}{oc}")
                ev = p.add("scalar", ACT(y1b[oc][:, :], y[:, 0:TG], AF.Identity,
                                         bias=pv[:, m1 * 8 + oc:m1 * 8 + oc + 1]),
                           deps=[mmy], tag=f"ey1{blk}{tg}{oc}")
                ev_prev[oc % 2] = ev
            evs = [ev_prev[0], ev_prev[1]]
            mm_s = mm_q = None
            for dc in range(8):
                q = p.add("scalar", ACT(sq[:, :], y1b[dc][:, :], AF.Square),
                          deps=evs + [mm_q], tag=f"isq{blk}{tg}{dc}")
                mm_s = p.add("tensor", MM(pbS1[0:1, 0:TG], ones_f[:, 0:1], y1b[dc][:, :], dc == 0, dc == 7),
                             deps=evs, tag=f"ims{blk}{tg}{dc}")
                mm_q = p.add("tensor", MM(pbS2[0:1, 0:TG], ones_f[:, 0:1], sq[:, :], dc == 0, dc == 7),
                             deps=[q], tag=f"imq{blk}{tg}{dc}")
            a1 = p.add("vector", TS1(t1[:, :], pbS1[0:1, 0:TG], 1.0 / D, ALU.mult), deps=[mm_s], tag=f"ia1{blk}{tg}")
            a2 = p.add("vector", TS1(t2[:, :], pbS2[0:1, 0:TG], 1.0 / D, ALU.mult), deps=[mm_q], tag=f"ia2{blk}{tg}")
            a3 = p.add("vector", TT(t3[:, :], t1[:, :], t1[:, :], ALU.mult), deps=[a1], tag=f"ia3{blk}{tg}")
            a4 = p.add("vector", TT(t2[:, :], t2[:, :], t3[:, :], ALU.subtract), deps=[a2, a3], tag=f"ia4{blk}{tg}")
            r1 = p.add("scalar", ACT(t2[:, :], t2[:, :], AF.Ln, bias=1e-5), deps=[a4], tag=f"ir1{blk}{tg}")
            r2 = p.add("scalar", ACT(t2[:, :], t2[:, :], AF.Exp, scale=-0.5), deps=[r1], tag=f"ir2{blk}{tg}")
            nm = p.add("vector", STT(t3[:, :], t1[:, :], -1.0, t2[:, :], ALU.mult, ALU.mult),
                       deps=[a3, r2], tag=f"inm{blk}{tg}")
            b1 = p.add("tensor", MM(pbB1[:, 0:TG], ones1f[0:1, :], t2[0:1, :], True, True), deps=[r2], tag=f"ib1{blk}{tg}")
            b2 = p.add("tensor", MM(pbB2[:, 0:TG], ones1f[0:1, :], t3[0:1, :], True, True), deps=[nm], tag=f"ib2{blk}{tg}")
            gd = {}
            prev = None
            for dc in range(8):
                w1 = p.add("vector", TT(ftmp[:, :], y1b[dc][:, :], pbB1[:, 0:TG], ALU.mult),
                           deps=evs + [b1, prev], tag=f"iw1{blk}{tg}{dc}")
                w2 = p.add("vector", TT(ftmp[:, :], ftmp[:, :], pbB2[:, 0:TG], ALU.add), deps=[w1, b2], tag=f"iw2{blk}{tg}{dc}")
                w3 = p.add("vector", TS(ftmp[:, :], ftmp[:, :], pv[:, 48 + blk * 8 + dc:48 + blk * 8 + dc + 1],
                                        pv[:, 72 + blk * 8 + dc:72 + blk * 8 + dc + 1], ALU.mult, ALU.add),
                           deps=[w2], tag=f"iw3{blk}{tg}{dc}")
                g = p.add("scalar", ACT(gb[dc][:, :], ftmp[:, :], AF.Gelu), deps=[w3], tag=f"ig{blk}{tg}{dc}")
                gd[dc] = g
                prev = g
            mm = None
            for dc in range(8):
                mm = p.add("tensor", MM(pbR[0:32, 0:TG], la_sb[:, (m2 * 8 + dc) * 32:(m2 * 8 + dc + 1) * 32],
                                        gb[dc][:, :], dc == 0, dc == 7), deps=[gd[dc], er], tag=f"r2{blk}{tg}{dc}")
            er2 = p.add("scalar", ACT(rb[:, :], pbR[0:32, 0:TG], AF.Copy), deps=[mm], tag=f"er2{blk}{tg}")
            evp2 = {0: evs[0], 1: evs[1]}
            for oc in range(8):
                y = pbY[oc % 2]
                for dc in range(8):
                    p.add("tensor", MM(y[:, 0:TG], wcur[dc][:, D + oc * 128:D + (oc + 1) * 128],
                                       gb[dc][:, :], dc == 0, False),
                          deps=[gd[dc], wl[dc][1], evp2[oc % 2]], tag=f"y2{blk}{tg}{oc}{dc}")
                mmy = p.add("tensor", MM(y[:, 0:TG], lb_sb[:, m2 * D + oc * 128:m2 * D + (oc + 1) * 128],
                                         rb[:, :], False, True), deps=[er2], tag=f"y2l{blk}{tg}{oc}")
                u1 = p.add("vector", TS1(ftmp2[:, :], y[:, 0:TG], pv[:, m2 * 8 + oc:m2 * 8 + oc + 1], ALU.add),
                           deps=[mmy], tag=f"u1{blk}{tg}{oc}")
                u2 = p.add("vector", TT(h[oc][:, s], ftmp2[:, :], h[oc][:, s], ALU.add),
                           deps=[u1] + h_w[oc], tag=f"u2{blk}{tg}{oc}")
                h_w[oc] = [u2]
                evp2[oc % 2] = u1
            p.fence()

    for tg in range(NTG):
        s = slice(tg * TG, (tg + 1) * TG)
        for dc in range(8):
            lx = p.add("sync", DMA(xtile[:, :], x_in[dc * 128:(dc + 1) * 128, s]), dma=True, tag=f"fx{tg}{dc}")
            fo = p.add("vector", STT(xtile[:, :], h[dc][:, s], 0.5, xtile[:, :], ALU.mult, ALU.add),
                       deps=[lx] + h_w[dc], tag=f"ff{tg}{dc}")
            st = p.add("sync", DMA(x_out[dc * 128:(dc + 1) * 128, s], xtile[:, :]), deps=[fo], dma=True, tag=f"fs{tg}{dc}")
            p.fence()
    if own:
        p.add("sync", lambda e: e.nop(), tag="final")
        p.emit(stack)
        stack.close()
        return nc
    p.fence()
    stack.close()
    return None


def build_scan(ctx=None, uid="", xin="x_in", xout="x_out"):
    import os
    CUT = 5
    own = ctx is None
    if own:
        nc = bass.Bass("TRN2", target_bir_lowering=False)
        p = Prog(nc, strict=True)
        ctx = Ctx(nc, p, {})
    nc, p = ctx.nc, ctx.p
    x_in = ctx.dram(xin, [D, TOK], F32, "ExternalInput" if xin == "x_in" else "Internal")
    w_in = ctx.dram("w_in", [8, 128, 128], F32, "ExternalInput")
    w_a = ctx.dram("w_a", [8, 128, 128], F32, "ExternalInput")
    pvec = ctx.dram("pvec2", [128, 24], F32, "ExternalInput")
    selm = ctx.dram("selm", [1, 1], F32, "ExternalInput")
    x_out = ctx.dram(xout, [D, TOK], F32, "ExternalOutput" if xout == "x_out" else "Internal")
    ccLi = ctx.dram("ccLi", [1, D], F32, "Internal")
    ccLo = ctx.dram("ccLo", [2, D], F32, "Internal")
    ccSi = ctx.dram("ccSi", [1, D], F32, "Internal")
    ccSo = ctx.dram("ccSo", [2, D], F32, "Internal")

    stack = contextlib.ExitStack()
    sb, ps, ones_bf, ones1f, ones_f, (sq, t1, t2, t3, tmpf) = mk_common(nc, stack, p, uid + "sc")

    xn = [sb(f"xn{i}", [128, TOK], F32) for i in range(8)]
    LX = [sb(f"LX{i}", [128, TOK], F32) for i in range(8)]
    wsb = sb("wsb", [128, 2 * 128], F32)
    pv = sb("pv2", [128, 24], F32)
    gx = sb("gx", [128, TOK], F32)
    ga = sb("ga", [128, TOK], F32)
    la = sb("la", [128, TOK], F32)
    at = sb("at", [128, TOK], F32)
    wt = sb("wt", [128, TOK], F32)
    ones_t = sb("ones_t", [128, TOK], F32)
    negkd = sb("negkd", [128, 8], F32)
    kda = sb("kda", [128, 8], F32)
    kdb = sb("kdb", [128, 8], F32)
    sel_sb = sb("sel_sb", [1, 1], F32)
    gl = sb("gl", [2, D], F32)
    cstrip = sb("cstrip", [1, D], F32)
    ccol = sb("ccol", [128, 8 * 2], F32)
    scol = sb("scol", [128, 8], F32)
    onef1 = sb("onef1", [1, 1], F32)
    xt2 = sb("xt2", [128, TG], F32)

    pbG, pbA = ps("pbG"), ps("pbA")
    pbS1, pbS2, pbB1, pbB2 = ps("pbS1b"), ps("pbS2b"), ps("pbB1b"), ps("pbB2b")
    lnbufs = (sq, t1, t2, t3, tmpf, ones_bf, ones_f, ones1f, pbS1, pbS2, pbB1, pbB2)

    p.add("vector", MEMSET(ones_t[:, :], 1.0), tag="onest")
    p.add("vector", MEMSET(onef1[:, :], 1.0), tag="onef1")
    lpv = p.add("sync", DMA(pv[:, :], pvec[:, :]), dma=True, tag="lpv")
    p.add("sync", DMA(sel_sb[:, :], selm[:, :]), dma=True, tag="lsel")
    s1 = p.add("scalar", ACT(kda[:, :], pv[:, 16:24], AF.Abs), deps=[lpv], tag="sp1")
    s2 = p.add("scalar", ACT(kda[:, :], kda[:, :], AF.Exp, scale=-1.0), deps=[s1], tag="sp2")
    s3 = p.add("scalar", ACT(kda[:, :], kda[:, :], AF.Ln, bias=1.0), deps=[s2], tag="sp3")
    s4 = p.add("vector", TS1(kdb[:, :], pv[:, 16:24], 0.0, ALU.max), deps=[lpv], tag="sp4")
    s5 = p.add("vector", TT(kda[:, :], kda[:, :], kdb[:, :], ALU.add), deps=[s3, s4], tag="sp5")
    s6 = p.add("vector", TS(negkd[:, :], kda[:, :], -1.0, -0.001, ALU.mult, ALU.add), deps=[s5], tag="sp6")
    p.fence()

    xdeps = {dc: [p.add("sync", DMA(xn[dc][:, :], x_in[dc * 128:(dc + 1) * 128, :]), dma=True, tag=f"lx{dc}")]
             for dc in range(8)}
    p.fence()
    emit_ln(p, lnbufs, xn, xdeps, xn, "ln2")
    p.fence()

    for hc in range(8 if CUT >= 2 else 0):
        cw1 = p.add("sync", DMA(wsb[:, 0:128], w_in[hc, :, :]), dma=True, tag=f"lw1{hc}")
        cw2 = p.add("sync", DMA(wsb[:, 128:256], w_a[hc, :, :]), deps=[cw1], dma=True, tag=f"lw2{hc}")
        for tg in range(NTG):
            s = slice(tg * TG, (tg + 1) * TG)
            m1 = p.add("tensor", MM(pbG[:, 0:TG], wsb[:, 0:128], xn[hc][:, s], True, True),
                       deps=[cw1], tag=f"mg{hc}{tg}")
            e1 = p.add("scalar", ACT(gx[:, s], pbG[:, 0:TG], AF.Sigmoid, bias=pv[:, hc:hc + 1]),
                       deps=[m1], tag=f"eg{hc}{tg}")
            m2 = p.add("tensor", MM(pbA[:, 0:TG], wsb[:, 128:256], xn[hc][:, s], True, True),
                       deps=[cw2], tag=f"ma{hc}{tg}")
            e2 = p.add("scalar", ACT(ga[:, s], pbA[:, 0:TG], AF.Sigmoid, bias=pv[:, 8 + hc:8 + hc + 1]),
                       deps=[m2], tag=f"ea{hc}{tg}")
            p.fence()
        v1a = p.add("vector", TS1(la[:, :], ga[:, :], negkd[:, hc:hc + 1], ALU.mult),
                    deps=[s6], tag=f"vlaa{hc}")
        v1 = p.add("vector", TS1(la[:, :], la[:, :], LOG1M, ALU.min),
                   deps=[v1a], tag=f"vla{hc}")
        v2 = p.add("vector", SCAN(LX[hc][:, :], ones_t[:, :], la[:, :], 0.0, ALU.mult, ALU.add),
                   deps=[v1], tag=f"scL{hc}")
        ctb = p.add("sync", DMA(ccLi[0:1, hc * 128:(hc + 1) * 128], LX[hc][:, TOK - 1:TOK]),
                    deps=[v2], dma=True, tag=f"ctL{hc}")
        v3 = p.add("vector", TT(LX[hc][:, :], LX[hc][:, :], la[:, :], ALU.subtract), deps=[v2, ctb], tag=f"exL{hc}")
        a1 = p.add("scalar", ACT(at[:, :], la[:, :], AF.Exp), deps=[v1, v3], tag=f"ea2{hc}")
        a2 = p.add("scalar", ACT(at[:, :], at[:, :], AF.Square), deps=[a1], tag=f"ea3{hc}")
        a3 = p.add("vector", TS(at[:, :], at[:, :], -1.0, 1.0, ALU.mult, ALU.add), deps=[a2], tag=f"ea4{hc}")
        a4 = p.add("vector", TS1(at[:, :], at[:, :], 1e-5, ALU.max), deps=[a3], tag=f"ea5{hc}")
        a5 = p.add("scalar", ACT(at[:, :], at[:, :], AF.Sqrt), deps=[a4], tag=f"ea6{hc}")
        u1 = p.add("vector", TT(xn[hc][:, :], xn[hc][:, :], gx[:, :], ALU.mult), deps=[a5, e1], tag=f"u1{hc}")
        u2 = p.add("vector", TT(xn[hc][:, :], xn[hc][:, :], at[:, :], ALU.mult), deps=[u1], tag=f"u2{hc}")
        p.fence()

    if CUT >= 3:
        if os.environ.get("SCAN_NOCC", "0") == "1":
            ccL = p.add("gpsimd", lambda e: e.nop(), tag="ccLnop")
        else:
            ccL = p.add("gpsimd", lambda e: e.collective_compute(
                "AllGather", ALU.bypass, replica_groups=[[0, 1], [2, 3], [4, 5], [6, 7]],
                ins=[ccLi[:, :]], outs=[ccLo[:, :]]), tag="ccL")
        lgl = p.add("sync", DMA(gl[:, :], ccLo[:, :]), deps=[ccL], dma=True, tag="lgl")
        sc0 = p.add("vector", TS1(cstrip[0:1, :], gl[0:1, :], sel_sb[0:1, 0:1], ALU.mult),
                    deps=[lgl], tag="selL")
        p.fence()
        cold = []
        for hc in range(8):
            cd = p.add("sync", DMA(ccol[:, 2 * hc:2 * hc + 1], cstrip[0:1, hc * 128:(hc + 1) * 128]),
                       deps=[sc0], dma=True, tag=f"cL{hc}")
            cold.append(cd)
        p.fence()
        for hc in range(8):
            p.add("vector", TS1(ccol[:, 2 * hc + 1:2 * hc + 2], ccol[:, 2 * hc:2 * hc + 1], -1.0, ALU.mult),
                  tag=f"cLn{hc}")
        p.fence()
    else:
        p.add("vector", MEMSET(ccol[:, :], 0.0), tag="zc")
        p.fence()

    ECL = 16.118096
    for hc in range(8 if CUT >= 4 else 0):
        lt0 = p.add("vector", TS1(la[:, :], LX[hc][:, :], ccol[:, 2 * hc:2 * hc + 1], ALU.add), tag=f"lt{hc}")
        lc1 = p.add("vector", TS1(at[:, :], la[:, :], -ECL, ALU.max), deps=[lt0], tag=f"lc1{hc}")
        pe = p.add("scalar", ACT(gx[:, :], at[:, :], AF.Exp), deps=[lc1], tag=f"pe{hc}")
        pc = p.add("vector", TS1(gx[:, :], gx[:, :], 1e-7, ALU.max), deps=[pe], tag=f"pc{hc}")
        lc2 = p.add("vector", TS(at[:, :], la[:, :], -1.0, ECL, ALU.mult, ALU.min), deps=[lt0, pe], tag=f"lc2{hc}")
        pi = p.add("scalar", ACT(ga[:, :], at[:, :], AF.Exp), deps=[lc2], tag=f"pi{hc}")
        pic = p.add("vector", TS1(ga[:, :], ga[:, :], 1e7, ALU.min), deps=[pi], tag=f"pic{hc}")
        w1 = p.add("vector", STT(wt[:, :], xn[hc][:, :], 1e-7, ga[:, :], ALU.add, ALU.mult),
                   deps=[pic], tag=f"w{hc}")
        sc = p.add("vector", SCAN(xn[hc][:, :], ones_t[:, :], wt[:, :], 0.0, ALU.mult, ALU.add),
                   deps=[w1], tag=f"scS{hc}")
        p.add("sync", DMA(ccSi[0:1, hc * 128:(hc + 1) * 128], xn[hc][:, TOK - 1:TOK]), deps=[sc],
              dma=True, tag=f"ctS{hc}")
        p.add("vector", COPY(LX[hc][:, :], gx[:, :]), deps=[pc, sc], tag=f"keepp{hc}")
        p.fence()

    if CUT >= 5:
        ccS = p.add("gpsimd", lambda e: e.collective_compute(
            "AllGather", ALU.bypass, replica_groups=[[0, 1], [2, 3], [4, 5], [6, 7]],
            ins=[ccSi[:, :]], outs=[ccSo[:, :]]), tag="ccS")
        lgs = p.add("sync", DMA(gl[:, :], ccSo[:, :]), deps=[ccS], dma=True, tag="lgs")
        sc1 = p.add("vector", TS1(cstrip[0:1, :], gl[0:1, :], sel_sb[0:1, 0:1], ALU.mult),
                    deps=[lgs], tag="selS")
        p.fence()
        for hc in range(8):
            p.add("sync", DMA(scol[:, hc:hc + 1], cstrip[0:1, hc * 128:(hc + 1) * 128]),
                  deps=[sc1], dma=True, tag=f"cS{hc}")
        p.fence()
    else:
        p.add("vector", MEMSET(scol[:, :], 0.0), tag="zs")
        p.fence()

    for hc in range(8 if CUT >= 5 else 0):
        hf = p.add("vector", STT(xn[hc][:, :], xn[hc][:, :], scol[:, hc:hc + 1], LX[hc][:, :],
                                 ALU.add, ALU.mult), tag=f"hf{hc}")
        for tg in range(NTG):
            s = slice(tg * TG, (tg + 1) * TG)
            lx = p.add("sync", DMA(xt2[:, :], x_in[hc * 128:(hc + 1) * 128, s]), dma=True, tag=f"fx{hc}{tg}")
            fo = p.add("vector", TT(xt2[:, :], xt2[:, :], xn[hc][:, s], ALU.add), deps=[lx, hf], tag=f"fa{hc}{tg}")
            p.add("sync", DMA(x_out[hc * 128:(hc + 1) * 128, s], xt2[:, :]), deps=[fo], dma=True, tag=f"fs{hc}{tg}")
            p.fence()
    if own:
        p.add("sync", lambda e: e.nop(), tag="final")
        p.emit(stack)
        stack.close()
        return nc
    p.fence()
    stack.close()
    return None


def build_ffn1(ctx=None, uid="", xin="x_in", xout="x_out", dq=True):
    HID = 3072
    own = ctx is None
    if own:
        nc = bass.Bass("TRN2", target_bir_lowering=False)
        p = Prog(nc, strict=True)
        ctx = Ctx(nc, p, {})
    nc, p = ctx.nc, ctx.p
    x_in = ctx.dram(xin, [D, TOK], F32, "ExternalInput" if xin == "x_in" else "Internal")
    wgT = ctx.dram("wgT", [D, HID], I32, "ExternalInput")
    wuT = ctx.dram("wuT", [D, HID], I32, "ExternalInput")
    wdT = ctx.dram("wdT", [HID, D], I32, "ExternalInput")
    x_out = ctx.dram(xout, [D, TOK], F32, "ExternalOutput" if xout == "x_out" else "Internal")
    wd_dram = ctx.dram("p3_wd", [HID, D], BF16, "Internal")
    wg_dram = ctx.dram("p3_wg", [D, HID], BF16, "Internal")
    wu_dram = ctx.dram("p3_wu", [D, HID], BF16, "Internal")

    stack = contextlib.ExitStack()
    sb, ps, ones_bf, ones1f, ones_f, (sq, t1, t2, t3, tmpf) = mk_common(nc, stack, p, uid + "f1")

    xg = [sb(f"xg{i}", [128, TG], F32) for i in range(8)]
    xnb = [sb(f"xb{i}", [128, TOK], BF16) for i in range(8)]
    wgb = [sb(f"wg{i}", [128, HID], BF16) for i in range(8)]
    wub = [sb(f"wu{i}", [128, HID], BF16) for i in range(8)]
    hid = [sb(f"hd{i}", [128, TG], BF16) for i in range(24)]
    itile = sb("it", [128, HID], I32)
    ftile = sb("ft", [128, D], F32)
    wdt = [sb(f"wdt{i}", [128, 128], BF16) for i in range(2)]
    sg = sb("sg", [128, TG], F32)
    xt = sb("xt", [128, TG], F32)
    btmp = sb("btmp", [128, D], BF16)

    pbG, pbU = ps("pbG"), ps("pbU")
    pbD = [ps("pbD0"), ps("pbD1")]
    pbS1, pbS2, pbB1, pbB2 = ps("pbS1c"), ps("pbS2c"), ps("pbB1c"), ps("pbB2c")
    lnbufs = (sq, t1, t2, t3, tmpf, ones_bf, ones_f, ones1f, pbS1, pbS2, pbB1, pbB2)

    def tl():
        return ("tensor", len(p.ops["tensor"]) - 1)

    if dq:
        for dc in range(8):
            l1 = p.add("sync", DMA(itile[:, :], wgT[dc * 128:(dc + 1) * 128, :]), dma=True, tag=f"lg{dc}")
            for t in range(3):
                cs = slice(t * D, (t + 1) * D)
                c1 = p.add("vector", COPY(ftile[:, :], itile[:, cs]), deps=[l1], tag=f"cg{dc}{t}")
                p.add("vector", TS1(wgb[dc][:, cs], ftile[:, :], 1.0 / 127.0, ALU.mult), deps=[c1], tag=f"sg{dc}{t}")
            p.add("sync", DMA(wg_dram[dc * 128:(dc + 1) * 128, :], wgb[dc][:, :]), dma=True, tag=f"wgs{dc}")
            p.fence()
            l2 = p.add("sync", DMA(itile[:, :], wuT[dc * 128:(dc + 1) * 128, :]), dma=True, tag=f"lu{dc}")
            for t in range(3):
                cs = slice(t * D, (t + 1) * D)
                c1 = p.add("vector", COPY(ftile[:, :], itile[:, cs]), deps=[l2], tag=f"cu{dc}{t}")
                p.add("vector", TS1(wub[dc][:, cs], ftile[:, :], 1.0 / 127.0, ALU.mult), deps=[c1], tag=f"su{dc}{t}")
            p.add("sync", DMA(wu_dram[dc * 128:(dc + 1) * 128, :], wub[dc][:, :]), dma=True, tag=f"wus{dc}")
            p.fence()
        for hc in range(24):
            l1 = p.add("sync", DMA(itile[:, 0:D], wdT[hc * 128:(hc + 1) * 128, :]), dma=True, tag=f"ld{hc}")
            c1 = p.add("vector", COPY(ftile[:, :], itile[:, 0:D]), deps=[l1], tag=f"cd{hc}")
            c2 = p.add("vector", TS1(btmp[:, :], ftile[:, :], 1.0 / 127.0, ALU.mult), deps=[c1], tag=f"sd{hc}")
            p.add("sync", DMA(wd_dram[hc * 128:(hc + 1) * 128, :], btmp[:, :]), deps=[c2], dma=True, tag=f"wd{hc}")
            p.fence()
    else:
        for dc in range(8):
            p.add("sync", DMA(wgb[dc][:, :], wg_dram[dc * 128:(dc + 1) * 128, :]), dma=True, tag=f"rg{dc}")
            p.add("sync", DMA(wub[dc][:, :], wu_dram[dc * 128:(dc + 1) * 128, :]), dma=True, tag=f"ru{dc}")
        p.fence()

    emit_ln_stream(p, lnbufs, xg, x_in, xnb, "ln3")
    p.fence()

    for tg in range(NTG):
        s = slice(tg * TG, (tg + 1) * TG)
        hdeps = {}
        evg = eva = None
        for hc in range(24):
            for dc in range(8):
                p.add("tensor", MM(pbG[:, 0:TG], wgb[dc][:, hc * 128:(hc + 1) * 128], xnb[dc][:, s],
                                   dc == 0, dc == 7), deps=[evg], tag=f"g{tg}{hc}{dc}")
            gmm = tl()
            for dc in range(8):
                p.add("tensor", MM(pbU[:, 0:TG], wub[dc][:, hc * 128:(hc + 1) * 128], xnb[dc][:, s],
                                   dc == 0, dc == 7), deps=[eva], tag=f"u{tg}{hc}{dc}")
            umm = tl()
            evg0 = p.add("scalar", ACT(sg[:, :], pbG[:, 0:TG], AF.Silu), deps=[gmm], tag=f"sl{tg}{hc}")
            hm = p.add("vector", TT(hid[hc][:, :], sg[:, :], pbU[:, 0:TG], ALU.mult),
                       deps=[evg0, umm], tag=f"hm{tg}{hc}")
            hdeps[hc] = hm
            evg = evg0
            eva = hm
        p.fence()
        dprev = {0: None, 1: None}
        wprev = {0: None, 1: None}
        for oc in range(8):
            y = pbD[oc % 2]
            for hc in range(24):
                bi = hc % 2
                wl = p.add("sync", DMA(wdt[bi][:, :], wd_dram[hc * 128:(hc + 1) * 128, oc * 128:(oc + 1) * 128]),
                           deps=[wprev[bi]], dma=True, tag=f"dl{tg}{oc}{hc}")
                p.add("tensor", MM(y[:, 0:TG], wdt[bi][:, :], hid[hc][:, :], hc == 0, hc == 23),
                      deps=[wl, hdeps[hc], dprev[oc % 2]], tag=f"d{tg}{oc}{hc}")
                wprev[bi] = tl()
            dmm = tl()
            lx = p.add("sync", DMA(xt[:, :], x_in[oc * 128:(oc + 1) * 128, s]), dma=True, tag=f"dx{tg}{oc}")
            fo = p.add("vector", STT(xt[:, :], y[:, 0:TG], 0.5, xt[:, :], ALU.mult, ALU.add),
                       deps=[lx, dmm], tag=f"df{tg}{oc}")
            p.add("sync", DMA(x_out[oc * 128:(oc + 1) * 128, s], xt[:, :]), deps=[fo], dma=True, tag=f"ds{tg}{oc}")
            dprev[oc % 2] = fo
            p.fence()
    if own:
        p.add("sync", lambda e: e.nop(), tag="final")
        p.emit(stack)
        stack.close()
        return nc
    p.fence()
    stack.close()
    return None


def build_attn(ctx=None, uid="", xin="x_in", xout="x_out"):
    own = ctx is None
    if own:
        nc = bass.Bass("TRN2", target_bir_lowering=False)
        p = Prog(nc)
        ctx = Ctx(nc, p, {})
    nc, p = ctx.nc, ctx.p
    x_in = ctx.dram(xin, [D, TOK], F32, "ExternalInput" if xin == "x_in" else "Internal")
    WT = ctx.dram("WT", [4, D, D], F32, "ExternalInput")
    pvec = ctx.dram("pvec4", [128, 32], F32, "ExternalInput")
    bvrow = ctx.dram("bvrow", [1, D], F32, "ExternalInput")
    relT = ctx.dram("relT", [64, TW], F32, "ExternalInput")
    idm = ctx.dram("idm", [128, 128], BF16, "ExternalInput")
    x_out = ctx.dram(xout, [D, TOK], F32, "ExternalOutput" if xout == "x_out" else "Internal")
    cc4i = ctx.dram("cc4i", [D, TOK], BF16, "Internal")
    cc4o = ctx.dram("cc4o", [2 * D, TOK], BF16, "Internal")
    pp0 = ctx.dram("pp0", [8, TOK * PWS + PW], BF16, "Internal")
    pp1 = ctx.dram("pp1", [8, TOK * PWS + PW], BF16, "Internal")

    stack = contextlib.ExitStack()
    sb, ps, ones_bf, ones1f, ones_f, (sq, t1, t2, t3, tmpf) = mk_common(nc, stack, p, uid + "at")

    xg = [sb(f"xg{i}", [128, TG], F32) for i in range(8)]
    p4 = [sb(f"p4_{i}", [128, TOK], BF16) for i in range(8)]
    pv = sb("pv4", [128, 32], F32)
    bqs = sb("bqs", [128, 8], F32)
    bvf = sb("bvf", [1, D], F32)
    bvb = sb("bvb", [1, D], BF16)
    ones1b = sb("ones1b", [1, 128], BF16)
    idb = sb("idb", [128, 128], BF16)
    ttb = sb("ttb", [128, TW], BF16)
    wg4 = [sb(f"wg4_{i}", [128, 256], BF16) for i in range(8)]
    wk4 = [sb(f"wk4_{i}", [128, 256], BF16) for i in range(8)]
    wv4 = [sb(f"wv4_{i}", [128, 256], BF16) for i in range(8)]
    wot = [sb(f"wot{i}", [128, 128], BF16) for i in range(8)]
    wtmp = sb("wtmp4", [128, 256], F32)
    wtmp2 = sb("wtmp2", [128, 256], F32)
    qhg = [sb(f"qhg{i}", [128, TOK], BF16) for i in range(2)]
    khg = [sb(f"khg{i}", [128, 2 * TOK], BF16) for i in range(2)]
    vhg = [sb(f"vhg{i}", [128, 4 * 65], BF16) for i in range(32)]
    g4 = [sb(f"g4_{i}", [128, TG], BF16) for i in range(8)]
    band = [sb(f"band{i}", [128, TOK], BF16) for i in range(4)]
    psb = sb("psb", [128, PWS], BF16)
    probs = [sb(f"pr{i}", [128, TG], BF16) for i in range(2)]
    attn = [sb(f"at{i}", [128, TOK], BF16) for i in range(8)]
    rs = sb("rs", [1, TG], F32)
    ao = sb("ao", [64, TG], F32)
    xt = sb("xt4", [128, TG], F32)

    pbA, pbB = ps("pbA"), ps("pbB")
    pbS = [ps("pbSa"), ps("pbSb")]
    pbO = ps("pbO")
    pbS1, pbS2, pbB1, pbB2 = pbA, pbB, pbS[0], pbS[1]
    lnbufs = (sq, t1, t2, t3, tmpf, ones_bf, ones_f, ones1f, pbS1, pbS2, pbB1, pbB2)

    def tl():
        return ("tensor", len(p.ops["tensor"]) - 1)

    p.add("vector", MEMSET(ones1b[:, :], 1.0), tag="ones1b")
    lpv = p.add("sync", DMA(pv[:, :], pvec[:, :]), dma=True, tag="lpv")
    p.add("vector", TS1(bqs[:, :], pv[:, 0:8], 0.125, ALU.mult), deps=[lpv], tag="bqs")
    lbv = p.add("sync", DMA(bvf[:, :], bvrow[:, :]), dma=True, tag="lbv")
    p.add("vector", COPY(bvb[:, :], bvf[:, :]), deps=[lbv], tag="cbv")
    lid = p.add("sync", DMA(idb[:, :], idm[:, :]), dma=True, tag="lid")
    prevc = None
    for t in range(24):
        w = min(256, TW - t * 256)
        lt_ = p.add("sync", DMA(wtmp2[0:64, 0:w], relT[:, t * 256:t * 256 + w]), deps=[prevc], dma=True, tag=f"ltt{t}")
        c1_ = p.add("vector", COPY(ttb[0:64, t * 256:t * 256 + w], wtmp2[0:64, 0:w]), deps=[lt_], tag=f"ctt{t}")
        prevc = p.add("vector", COPY(ttb[64:128, t * 256:t * 256 + w], wtmp2[0:64, 0:w]), deps=[lt_, c1_], tag=f"ct2{t}")
    p.fence()

    emit_ln_stream(p, lnbufs, xg, x_in, p4, "ln4")
    for dc in range(8):
        p.add("sync", DMA(cc4i[dc * 128:(dc + 1) * 128, :], p4[dc][:, :]), dma=True, tag=f"sp4{dc}")
    p.fence()
    cc = p.add("gpsimd", lambda e: e.collective_compute(
        "AllGather", ALU.bypass, replica_groups=[[0, 1], [2, 3], [4, 5], [6, 7]],
        ins=[cc4i[:, :]], outs=[cc4o[:, :]]), tag="cc4")
    p.fence()

    for g in range(4):
        fs = slice(g * 256, (g + 1) * 256)
        for ec in range(8):
            l1 = p.add("sync", DMA(wtmp[:, :], WT[0, ec * 128:(ec + 1) * 128, fs]), dma=True, tag=f"lwq{g}{ec}")
            c1 = p.add("vector", COPY(wg4[ec][:, :], wtmp[:, :]), deps=[l1], tag=f"cwq{g}{ec}")
            l2 = p.add("sync", DMA(wtmp[:, :], WT[1, ec * 128:(ec + 1) * 128, fs]), deps=[c1], dma=True, tag=f"lwk{g}{ec}")
            c2 = p.add("vector", COPY(wk4[ec][:, :], wtmp[:, :]), deps=[l2], tag=f"cwk{g}{ec}")
            l3 = p.add("sync", DMA(wtmp[:, :], WT[2, ec * 128:(ec + 1) * 128, fs]), deps=[c2], dma=True, tag=f"lwv{g}{ec}")
            p.add("vector", COPY(wv4[ec][:, :], wtmp[:, :]), deps=[l3], tag=f"cwv{g}{ec}")
            p.fence()
        for fc in range(2):
            for tg in range(NTG):
                s = slice(tg * TG, (tg + 1) * TG)
                for ec in range(8):
                    p.add("tensor", MM(pbA[:, 0:TG], wg4[ec][:, fc * 128:(fc + 1) * 128], p4[ec][:, s],
                                       ec == 0, ec == 7), tag=f"q{g}{fc}{tg}{ec}")
                p.add("scalar", ACT(qhg[fc][:, s], pbA[:, 0:TG], AF.Identity,
                                    bias=bqs[:, g * 2 + fc:g * 2 + fc + 1], scale=0.125),
                      deps=[tl()], tag=f"eq{g}{fc}{tg}")
                p.fence()
        for jg in range(8):
            half, col = jg // 4, (jg % 4) * TG
            scol_ = slice(col, col + TG)
            for ec in range(8):
                p.add("sync", DMA(g4[ec][:, :], cc4o[half * D + ec * 128:half * D + (ec + 1) * 128, scol_]),
                      deps=[cc], dma=True, tag=f"lg4{g}{jg}{ec}")
            p.fence()
            for fc in range(2):
                for ec in range(8):
                    p.add("tensor", MM(pbA[:, 0:TG], wk4[ec][:, fc * 128:(fc + 1) * 128], g4[ec][:, :],
                                       ec == 0, ec == 7), tag=f"k{g}{jg}{fc}{ec}")
                p.add("scalar", ACT(khg[fc][:, jg * TG:(jg + 1) * TG], pbA[:, 0:TG], AF.Identity,
                                    bias=pv[:, 8 + g * 2 + fc:8 + g * 2 + fc + 1]), deps=[tl()], tag=f"ek{g}{jg}{fc}")
                p.fence()
            for sub in range(4):
                jt = jg * 4 + sub
                for ec in range(8):
                    p.add("tensor", MM(pbB[:, 0:256], g4[ec][:, sub * 128:(sub + 1) * 128], wv4[ec][:, :],
                                       ec == 0, False), tag=f"v{g}{jt}{ec}")
                p.add("tensor", MM(pbB[:, 0:256], ones1b[:, :], bvb[0:1, fs], False, True), tag=f"vb{g}{jt}")
                vbm = tl()
                ms = p.add("vector", MEMSET(vhg[jt].rearrange("p (h u) -> p h u", u=65)[:, :, 64:65], 1.0),
                           tag=f"vo{g}{jt}")
                p.add("scalar", (lambda jt=jt: lambda e: e.copy(
                    vhg[jt].rearrange("p (h u) -> p h u", u=65)[:, :, 0:64],
                    pbB[:, 0:256].rearrange("p (h u) -> p h u", u=64)))(),
                    deps=[vbm, ms], tag=f"ev{g}{jt}")
                p.fence()
        for hh in range(4):
            hglob = g * 4 + hh
            qsl = qhg[hh // 2][64 * (hh % 2):64 * (hh % 2) + 64, :]
            for ib in range(16):
                c0 = 1920 - 128 * ib
                evd = []
                tsl = ttb[64 * (hh % 2):64 * (hh % 2) + 64, :]
                for ch in range(9):
                    wch = min(512, PW - ch * 512)
                    mm = p.add("tensor", MM(pbS[ch % 2][0:128, 0:wch], qsl[:, ib * 128:(ib + 1) * 128],
                                            tsl[:, c0 + ch * 512:c0 + ch * 512 + wch], True, True),
                               deps=[evd[ch - 2]] if ch >= 2 else [], tag=f"pp{hglob}{ib}{ch}")
                    ev = p.add("vector", COPY(psb[:, ch * 512:ch * 512 + wch], pbS[ch % 2][0:128, 0:wch]),
                               deps=[mm], tag=f"pe{hglob}{ib}{ch}")
                    evd.append(ev)
                ppt = pp0 if hglob < 8 else pp1
                p.add("sync", DMA(ppt[hglob % 8, ib * 128 * PWS:(ib + 1) * 128 * PWS]
                                  .rearrange("(r u) -> r u", u=PWS)[:, :], psb[:, :]),
                      deps=evd, dma=True, tag=f"pw{hglob}{ib}")
                p.fence()
        for hh in range(4):
            hglob = g * 4 + hh
            fc, pb_ = hh // 2, 64 * (hh % 2)
            achunk, abase = hglob // 2, 64 * (hglob % 2)
            ksl = khg[fc][pb_:pb_ + 64, :]
            qsl = qhg[fc][pb_:pb_ + 64, :]
            for ig in range(4):
                pvd = None
                exh = {}
                for jh in range(2):
                    bd = []
                    for ibl in range(4):
                        ib = ig * 4 + ibl
                        ppt = pp0 if hglob < 8 else pp1
                        flat = ppt[hglob % 8, ib * 128 * PWS + 127 + jh * TOK:
                                   ib * 128 * PWS + 127 + jh * TOK + 128 * PW]
                        sk = flat.rearrange("(r j) -> r j", j=PW)[:, 0:TOK]
                        bd.append(p.add("sync", DMA(band[ibl][:, 0:TOK], sk),
                                        deps=[pvd, exh.get(16 * jh - 1), exh.get(16 * jh - 2)],
                                        dma=True, tag=f"bd{hglob}{ig}{jh}{ibl}"))
                    for jt in range(16 * jh, 16 * jh + 16):
                        jl = jt - 16 * jh
                        y = pbS[jt % 2]
                        p.add("tensor", MM(y[:, 0:TG], ksl[:, jt * 128:(jt + 1) * 128], qsl[:, ig * TG:(ig + 1) * TG],
                                           True, False), deps=[pvd, exh.get(jt - 2)], tag=f"s{hglob}{ig}{jt}")
                        for ibl in range(4):
                            p.add("tensor", MM(y[:, ibl * 128:(ibl + 1) * 128], band[ibl][:, jl * 128:(jl + 1) * 128],
                                               idb[:, :], False, ibl == 3), deps=[bd[ibl], lid], tag=f"i{hglob}{ig}{jt}{ibl}")
                        ex = p.add("scalar", ACT(probs[jt % 2][:, :], y[:, 0:TG], AF.Exp),
                                   deps=[tl()], tag=f"x{hglob}{ig}{jt}")
                        exh[jt] = ex
                        pvd = p.add("tensor", MM(pbO[0:65, 0:TG], vhg[jt][:, hh * 65:(hh + 1) * 65],
                                                 probs[jt % 2][:, :], jt == 0, jt == 31), deps=[ex], tag=f"pv{hglob}{ig}{jt}")
                c1 = p.add("scalar", SCOPY(rs[0:1, :], pbO[64:65, 0:TG]), deps=[pvd], tag=f"nr{hglob}{ig}")
                c2 = p.add("vector", (lambda: lambda e: e.reciprocal(rs[0:1, :], rs[0:1, :]))(),
                           deps=[c1], tag=f"nv{hglob}{ig}")
                c3 = p.add("tensor", MM(pbB[0:64, 0:TG], ones1f[0:1, 0:64], rs[0:1, :], True, True),
                           deps=[c2], tag=f"nb{hglob}{ig}")
                c4 = p.add("scalar", SCOPY(ao[:, :], pbO[0:64, 0:TG]), deps=[pvd], tag=f"na{hglob}{ig}")
                c5 = p.add("vector", TT(attn[achunk][abase:abase + 64, ig * TG:(ig + 1) * TG], ao[:, :],
                                        pbB[0:64, 0:TG], ALU.mult), deps=[c3, c4], tag=f"nm{hglob}{ig}")
                p.fence()
    for fc in range(8):
        for ec in range(8):
            lw = p.add("sync", DMA(wtmp2[:, 0:128], WT[3, ec * 128:(ec + 1) * 128, fc * 128:(fc + 1) * 128]),
                       dma=True, tag=f"lwo{fc}{ec}")
            p.add("vector", COPY(wot[ec][:, :], wtmp2[:, 0:128]), deps=[lw], tag=f"cwo{fc}{ec}")
            p.fence()
        for tg in range(NTG):
            s = slice(tg * TG, (tg + 1) * TG)
            for ec in range(8):
                p.add("tensor", MM(pbA[:, 0:TG], wot[ec][:, :], attn[ec][:, s], ec == 0, ec == 7),
                      tag=f"o{tg}{fc}{ec}")
            eo = p.add("scalar", ACT(tmpf[:, :], pbA[:, 0:TG], AF.Identity, bias=pv[:, 24 + fc:24 + fc + 1]),
                       deps=[tl()], tag=f"eo{tg}{fc}")
            lx = p.add("sync", DMA(xt[:, :], x_in[fc * 128:(fc + 1) * 128, s]), dma=True, tag=f"ox{tg}{fc}")
            fo = p.add("vector", TT(xt[:, :], xt[:, :], tmpf[:, :], ALU.add), deps=[eo, lx], tag=f"of{tg}{fc}")
            p.add("sync", DMA(x_out[fc * 128:(fc + 1) * 128, s], xt[:, :]), deps=[fo], dma=True, tag=f"os{tg}{fc}")
            p.fence()
    if own:
        p.add("sync", lambda e: e.nop(), tag="final")
        p.emit(stack)
        stack.close()
        return nc
    p.fence()
    stack.close()
    return None


def build_mega():
    nc = bass.Bass("TRN2", target_bir_lowering=False)
    p = Prog(nc, strict=True)
    ctx = Ctx(nc, p, {})
    build_ffn0(ctx, uid="a", xin="x_in", xout="xs1", dq=True)
    build_scan(ctx, uid="b", xin="xs1", xout="xs2")
    build_ffn0(ctx, uid="c", xin="xs2", xout="xs3", dq=False)
    build_ffn1(ctx, uid="d", xin="xs3", xout="xs4", dq=True)
    p.strict = False
    build_attn(ctx, uid="e", xin="xs4", xout="xs5")
    p.strict = True
    build_ffn1(ctx, uid="f", xin="xs5", xout="x_out", dq=False)
    p.fence()
    p.add("sync", lambda e: e.nop(), tag="final")
    st = contextlib.ExitStack()
    p.emit(st)
    st.close()
    return nc


_NC_CACHE = {}


def _get(name, builder):
    if name not in _NC_CACHE:
        _NC_CACHE[name] = builder()
    return _NC_CACHE[name]


def _launch(nc, in_maps):
    res = run_bass_kernel_spmd(nc, in_maps, core_ids=list(range(8)))
    return res.results


def kernel(L0_attn_x, L0_attn_a_param, L0_attn_w_in, L0_attn_b_in, L0_attn_w_a, L0_attn_b_a,
           L0_ffn_qweights, L0_ffn_scales, L0_ffn_biases, L0_ffn_lora_a, L0_ffn_lora_b,
           L0_ffn_ln_gamma, L0_ffn_ln_beta,
           L1_attn_Wq, L1_attn_bq, L1_attn_Wk, L1_attn_bk, L1_attn_Wv, L1_attn_bv,
           L1_attn_Wo, L1_attn_bo, L1_attn_rel_table,
           L1_ffn_w_gate, L1_ffn_w_up, L1_ffn_w_down):
    x = np.ascontiguousarray(np.asarray(L0_attn_x, dtype=np.float32))

    def shards(xfull):
        out = []
        for c in range(8):
            b, hhalf = c // 2, c % 2
            out.append(np.ascontiguousarray(xfull[b, hhalf * TOK:(hhalf + 1) * TOK, :].T))
        return out

    xs = shards(x)

    qT = np.ascontiguousarray(np.asarray(L0_ffn_qweights).transpose(0, 2, 1)).astype(np.int32)
    sr = np.repeat(np.asarray(L0_ffn_scales, dtype=np.float32), 16, axis=2)
    srT = np.ascontiguousarray(sr.transpose(0, 2, 1))
    laT = np.ascontiguousarray(np.asarray(L0_ffn_lora_a, dtype=np.float32).transpose(0, 2, 1))
    lbT = np.ascontiguousarray(np.asarray(L0_ffn_lora_b, dtype=np.float32).transpose(0, 2, 1))
    pv1 = np.zeros((128, 96), np.float32)
    pv1[:, 0:48] = np.asarray(L0_ffn_biases, np.float32).reshape(6, 8, 128).transpose(2, 0, 1).reshape(128, 48)
    pv1[:, 48:72] = np.asarray(L0_ffn_ln_gamma, np.float32).reshape(3, 8, 128).transpose(2, 0, 1).reshape(128, 24)
    pv1[:, 72:96] = np.asarray(L0_ffn_ln_beta, np.float32).reshape(3, 8, 128).transpose(2, 0, 1).reshape(128, 24)

    nc1 = _get("mega", build_mega)
    pv2 = np.zeros((128, 24), np.float32)
    pv2[:, 0:8] = np.asarray(L0_attn_b_in, np.float32).T
    pv2[:, 8:16] = np.asarray(L0_attn_b_a, np.float32).T
    pv2[:, 16:24] = np.asarray(L0_attn_a_param, np.float32).reshape(8, 128).T
    wgT = np.ascontiguousarray(np.asarray(L1_ffn_w_gate).T).astype(np.int32)
    wuT = np.ascontiguousarray(np.asarray(L1_ffn_w_up).T).astype(np.int32)
    wdT = np.ascontiguousarray(np.asarray(L1_ffn_w_down).T).astype(np.int32)
    WTs = np.stack([np.ascontiguousarray(np.asarray(w, np.float32).T) for w in
                    (L1_attn_Wq, L1_attn_Wk, L1_attn_Wv, L1_attn_Wo)])
    pv4 = np.zeros((128, 32), np.float32)
    for i, b in enumerate((L1_attn_bq, L1_attn_bk, L1_attn_bv, L1_attn_bo)):
        pv4[:, i * 8:(i + 1) * 8] = np.asarray(b, np.float32).reshape(8, 128).T
    rel = np.asarray(L1_attn_rel_table, np.float32)
    maps = []
    for c in range(8):
        sel = np.zeros((1, 1), np.float32)
        if c % 2 == 1:
            sel[0, 0] = 1.0
        off = (c % 2) * TOK
        idx = np.clip(off + 3070 - np.arange(TW), 0, 2046)
        relT = np.ascontiguousarray(rel[idx].T)
        maps.append(dict(x_in=xs[c], qT=qT, srT=srT, laT=laT, lbT=lbT, pvec=pv1,
                         w_in=np.asarray(L0_attn_w_in, np.float32),
                         w_a=np.asarray(L0_attn_w_a, np.float32), pvec2=pv2, selm=sel,
                         wgT=wgT, wuT=wuT, wdT=wdT, WT=WTs, pvec4=pv4,
                         bvrow=np.asarray(L1_attn_bv, np.float32).reshape(1, D),
                         relT=relT, idm=np.eye(128).astype(BF)))
    r = _launch(nc1, maps)
    x6 = [np.asarray(r[c]["x_out"]) for c in range(8)]

    out = np.empty((4, 4096, D), np.float32)
    for c in range(8):
        b, hhalf = c // 2, c % 2
        out[b, hhalf * TOK:(hhalf + 1) * TOK, :] = x6[c].T
    return out
